# version 25
# baseline (speedup 1.0000x reference)
import numpy as np
import concourse.bass as bass
import concourse.mybir as mybir
from concourse.bass_utils import run_bass_kernel_spmd
from contextlib import ExitStack

F32 = mybir.dt.float32
BF16 = mybir.dt.bfloat16
AX = mybir.AxisListType.X
AF = mybir.ActivationFunctionType
OP = mybir.AluOpType

UNITS = 4
SLABW = 256
NSLAB = 32
EPS = 1e-5

DEBUG_TAPS = {}


def build_program(taps=None, stop_after=None):
    taps = dict(taps or {})
    nc = bass.Bass("TRN2", target_bir_lowering=False)

    x_d = nc.dram_tensor("x", [2048, 1024], F32, kind="ExternalInput").ap()
    wqkv_d = nc.dram_tensor("wqkv", [24, 128, 8, 128], F32, kind="ExternalInput").ap()
    bqkv_d = nc.dram_tensor("bqkv", [3072], F32, kind="ExternalInput").ap()
    wot_d = nc.dram_tensor("wot", [1024, 1024], F32, kind="ExternalInput").ap()
    bo_d = nc.dram_tensor("bo", [1, 1024], F32, kind="ExternalInput").ap()
    hh_d = nc.dram_tensor("hh", [8, 128, 8, 128], BF16, kind="ExternalInput").ap()
    wf_d = nc.dram_tensor("wf", [1024, 1024], F32, kind="ExternalInput").ap()
    i128f_d = nc.dram_tensor("i128f", [128, 128], F32, kind="ExternalInput").ap()
    i128b_d = nc.dram_tensor("i128b", [128, 128], BF16, kind="ExternalInput").ap()
    i64d_d = nc.dram_tensor("i64d", [128, 64], F32, kind="ExternalInput").ap()
    y_d = nc.dram_tensor("y", [2048, 1024], F32, kind="ExternalOutput").ap()
    tpt_d = nc.dram_tensor("tpt_scratch", [8, 128, 2048], BF16).ap()

    tap_d = {name: nc.dram_tensor(
                 "tap_" + name,
                 list(sh[0]) if isinstance(sh[0], (tuple, list)) else list(sh),
                 sh[1] if isinstance(sh[0], (tuple, list)) else F32,
                 kind="ExternalOutput").ap()
             for name, sh in taps.items()}

    ctx = ExitStack()
    with ctx:
        sbt = lambda n, s, d=F32: ctx.enter_context(nc.sbuf_tensor(n, s, d))
        pst = lambda n, s, d=F32: ctx.enter_context(nc.psum_tensor(n, s, d))
        mksem = lambda n: ctx.enter_context(nc.semaphore(n))

        x_tm = sbt("x_tm", [128, 4096])
        qb = sbt("qb", [128, 4096])
        kb = sbt("kb", [128, 4096])
        vb = sbt("vb", [128, 4096])
        vp = sbt("vp", [128, 64 * 65], BF16)
        vplo = sbt("vplo", [128, 64 * 65], BF16)
        QAhi = sbt("QAhi", [128, 8192], BF16)
        QAlo = sbt("QAlo", [128, 8192], BF16)
        KAhi = sbt("KAhi", [128, 8192], BF16)
        KAlo = sbt("KAlo", [128, 8192], BF16)
        etsl = sbt("etsl", [128, 2048])
        etslh = sbt("etslh", [128, 2048], BF16)
        etsll = sbt("etsll", [128, 2048], BF16)
        macc = sbt("macc", [128, 16])
        negm = sbt("negm", [128, 1])
        negmh = sbt("negmh", [128, 4], BF16)
        negp = sbt("negp", [128, 8], BF16)
        negml = sbt("negml", [128, 4], BF16)
        zb = sbt("zb", [128, 4096])
        wst = sbt("wst", [128, 2048])
        bqc = sbt("bqc", [128, 24])
        bor = sbt("bor", [1, 1024])
        onesb = sbt("onesb", [128, 128])
        i128f = sbt("i128f_s", [128, 128])
        i128b = sbt("i128b_s", [128, 128], BF16)
        i64d = sbt("i64d_s", [128, 64])
        negB = sbt("negB", [128, 1])
        srow = sbt("srow", [128, 512])
        rbc = sbt("rbc", [64, 512])
        ztmp = sbt("ztmp", [64, 1024])
        sc1 = sbt("sc1", [1, 1])
        sc2 = sbt("sc2", [1, 1])
        sc3 = sbt("sc3", [1, 1])
        mvst = sbt("mvst", [128, 12])
        mv = sbt("mv", [128, 2])
        rstd = sbt("rstd", [128, 1])
        mcol = sbt("mcol", [128, 1])
        sxc = sbt("sxc", [128, 4])
        rq = sbt("rq", [128, 1])

        y1 = qb[:, 0:4096]
        wfa = vb[:, 0:1024]
        wfs = vb[:, 1024:2048]
        qx_tm = vb[:, 1024:3072].bitcast(BF16)
        lad_a = vb[:, 3072:3584].bitcast(BF16)
        lad_b = vb[:, 3584:4096].bitcast(BF16)
        xT = QAhi[:].bitcast(F32)
        y1T = QAhi[:].bitcast(F32)
        xh_fm = QAlo[:].bitcast(F32)
        xh_tm = KAhi[:].bitcast(F32)
        qxT = KAlo[:].bitcast(F32)
        qxTb = KAlo[:, 0:4096]
        tstage = KAhi[:].bitcast(F32)

        pA = pst("pA", [128, 1024])
        pB = pst("pB", [128, 1024])
        pC = pst("pC", [128, 512])
        pD = pst("pD", [128, 512])
        pE = pst("pE", [128, 512])
        pF = pst("pF", [128, 512])
        BANK8 = [pA[:, 0:512], pA[:, 512:1024], pB[:, 0:512], pB[:, 512:1024],
                 pC[:, 0:512], pD[:, 0:512], pE[:, 0:512], pF[:, 0:512]]
        QUAD = [pA[:, 0:512], pA[:, 512:1024], pB[:, 0:512], pB[:, 512:1024]]
        pFb = pF[:].bitcast(BF16)

        s_ld = mksem("s_ld")
        s_gp = mksem("s_gp")
        s_te = mksem("s_te")
        s_dv = mksem("s_dv")
        s_ac = mksem("s_ac")
        s_gc = mksem("s_gc")
        s_st = mksem("s_st")

        class C:
            ld = 0; gp = 0; te = 0; dv = 0; ac = 0; gc = 0; st = 0

        plan = {k: [] for k in ("sync", "gpsimd", "tensor", "vector", "scalar")}

        def on(engine):
            def deco(fn):
                plan[engine].append(fn)
            return deco

        def dve_fence():
            @on("vector")
            def _(e, tgt=C.dv):
                e.wait_ge(s_dv, tgt)

        def tap(name, src):
            if name not in tap_d:
                return
            @on("sync")
            def _(e, src=src, dst=tap_d[name], dvv=C.dv, acv=C.ac, tev=C.te,
                  gpv=C.gp):
                e.wait_ge(s_dv, dvv)
                e.wait_ge(s_ac, acv)
                e.wait_ge(s_te, tev)
                if gpv:
                    e.wait_ge(s_gp, gpv)
                e.dma_start(dst, src).then_inc(s_st, 16)
            C.st += 16

        wst_user = {0: 0, 1: 0}

        def wst_load(half, src_ap, gate_dv=0, bf=False):
            wslice = wst[:, half * 1024:(half + 1) * 1024]
            if bf:
                nfree = 1
                for d_ in src_ap.shape[1:]:
                    nfree *= d_
                dst = wslice.bitcast(BF16)[:, 0:nfree]
            else:
                dst = wslice
            @on("sync")
            def _(e, dst=dst, src_ap=src_ap, need=wst_user[half],
                  gdv=gate_dv):
                if need:
                    e.wait_ge(s_te, need)
                if gdv:
                    e.wait_ge(s_dv, gdv)
                e.dma_start(dst, src_ap).then_inc(s_ld, 16)
            C.ld += 16
            return wslice, C.ld

        def emit_ln(src_ap, dst_ap, pre_dv, pre_ac=0, pre_st=0):
            @on("vector")
            def _(e, src_ap=src_ap, pre=pre_dv, pac=pre_ac):
                e.wait_ge(s_dv, pre)
                if pac:
                    e.wait_ge(s_ac, pac)
                e.bn_stats(mvst[:, 0:6], src_ap[:, 0:512]).then_inc(s_dv, 1)
                e.bn_stats(mvst[:, 6:12], src_ap[:, 512:1024]).then_inc(s_dv, 1)
            C.dv += 2
            dve_fence()
            @on("vector")
            def _(e):
                e.bn_aggr(mv[:], mvst[:]).then_inc(s_dv, 1)
            C.dv += 1
            dve_fence()
            @on("vector")
            def _(e):
                e.tensor_scalar(mcol[:], mv[:, 1:2], EPS, None, OP.add) \
                    .then_inc(s_dv, 1)
            C.dv += 1
            f = C.dv
            @on("scalar")
            def _(e, f=f):
                e.wait_ge(s_dv, f)
                e.activation(rstd[:], mcol[:], AF.Sqrt).then_inc(s_ac, 1)
            C.ac += 1
            fa = C.ac
            @on("vector")
            def _(e, fa=fa):
                e.wait_ge(s_ac, fa)
                e.reciprocal(mcol[:], rstd[:]).then_inc(s_dv, 1)
            C.dv += 1
            dve_fence()
            @on("vector")
            def _(e, src_ap=src_ap, dst_ap=dst_ap, pst_=pre_st):
                if pst_:
                    e.wait_ge(s_st, pst_)
                e.tensor_scalar(dst_ap, src_ap, mv[:, 0:1], mcol[:],
                                OP.subtract, OP.mult).then_inc(s_dv, 1)
            C.dv += 1
            return C.dv

        class _Stop(Exception):
            pass

        def ckpt(name, u=0):
            if stop_after is not None and name == stop_after and u == 0:
                raise _Stop()

        def emit_all():
            @on("sync")
            def _(e):
                e.dma_start(i128f[:], i128f_d).then_inc(s_ld, 16)
                e.dma_start(i128b[:], i128b_d).then_inc(s_ld, 16)
                e.dma_start(i64d[:], i64d_d).then_inc(s_ld, 16)
                e.dma_start(bor[:], bo_d).then_inc(s_ld, 16)
                e.dma_start(bqc[:], bqkv_d.rearrange("(t p) -> p t", p=128),
                            allow_slow_non_contiguous=True).then_inc(s_ld, 16)
            C.ld += 5 * 16
            init_ld = C.ld

            @on("gpsimd")
            def _(e):
                e.memset(onesb[:], 1.0).then_inc(s_gc, 1)
                e.memset(vp[:], 1.0).then_inc(s_gc, 1)
                e.memset(vplo[:], 0.0).then_inc(s_gc, 1)
            C.gc += 3
            ones_gc = C.gc

            prep_dv_of = {}
            prep_tr_te = {}
            prep_spill_ld = {}
            for ot in range(8):
                half = ot % 2
                wslice = wst[:, half * 1024:(half + 1) * 1024]
                @on("sync")
                def _(e, ot=ot, wslice=wslice, need=prep_dv_of.get(ot - 2, 0)):
                    if need:
                        e.wait_ge(s_dv, need)
                    e.dma_start(wslice, wf_d[128 * ot:128 * (ot + 1), :]) \
                        .then_inc(s_ld, 16)
                C.ld += 16
                ld_here = C.ld

                @on("scalar")
                def _(e, wslice=wslice, ld=ld_here,
                      trp=prep_tr_te.get(ot - 1, 0)):
                    e.wait_ge(s_ld, ld)
                    if trp:
                        e.wait_ge(s_te, trp)
                    e.activation(wfa, wslice, AF.Abs).then_inc(s_ac, 1)
                C.ac += 1
                abs_ac = C.ac

                @on("vector")
                def _(e, abs_ac=abs_ac):
                    e.wait_ge(s_ac, abs_ac)
                    e.tensor_reduce(rstd[:], wfa, AX, OP.add).then_inc(s_dv, 1)
                C.dv += 1
                dve_fence()
                @on("vector")
                def _(e):
                    e.tensor_scalar(rstd[:], rstd[:], 1.0 / 1024.0, EPS,
                                    OP.mult, OP.max).then_inc(s_dv, 1)
                C.dv += 1
                dve_fence()
                @on("vector")
                def _(e):
                    e.reciprocal(mcol[:], rstd[:]).then_inc(s_dv, 1)
                C.dv += 1
                dve_fence()
                @on("vector")
                def _(e, wslice=wslice):
                    e.tensor_scalar(wfs, wslice, mcol[:], None, OP.mult) \
                        .then_inc(s_dv, 1)
                C.dv += 1
                prep_dv_of[ot] = C.dv
                @on("vector")
                def _(e):
                    e.tensor_scalar(wfa, wfs, -0.5, None, OP.is_lt).then_inc(s_dv, 1)
                    e.scalar_tensor_tensor(wfs, wfs, 0.5, wfa, OP.is_gt,
                                           OP.subtract).then_inc(s_dv, 1)
                    e.tensor_scalar(wfa, wfs, rstd[:], None, OP.mult) \
                        .then_inc(s_dv, 1)
                C.dv += 3
                tp_dv = C.dv

                for g in range(2):
                    @on("tensor")
                    def _(e, g=g, tp_dv=tp_dv, init=init_ld, prior=C.dv):
                        e.wait_ge(s_ld, init)
                        e.wait_ge(s_dv, max(tp_dv, prior))
                        for i in range(4):
                            cc = g * 4 + i
                            mmi = e.matmul(pF[:, 128 * i:128 * (i + 1)],
                                           wfa[:, 128 * cc:128 * (cc + 1)], i128f[:],
                                           start=True, stop=True, is_transpose=True)
                            if i == 3:
                                mmi.then_inc(s_te, 1)
                    C.te += 1
                    tr_te = C.te
                    prep_tr_te[ot] = C.te
                    @on("vector")
                    def _(e, ot=ot, g=g, tr=tr_te,
                          sp=prep_spill_ld.get(ot // 4 - 1, 0)):
                        e.wait_ge(s_te, tr)
                        if sp:
                            e.wait_ge(s_ld, sp)
                        tb = tstage.bitcast(BF16)
                        dsth = tb.rearrange("p (h cc o) -> p h cc o", h=2, cc=8) \
                            [:, 0, g * 4:(g + 1) * 4,
                             128 * (ot % 4):128 * (ot % 4 + 1)]
                        dstl = tb.rearrange("p (h cc o) -> p h cc o", h=2, cc=8) \
                            [:, 1, g * 4:(g + 1) * 4,
                             128 * (ot % 4):128 * (ot % 4 + 1)]
                        src = pF[:].rearrange("p (i o) -> p i o", i=4)
                        e.tensor_copy(dsth, src).then_inc(s_dv, 1)
                        e.scalar_tensor_tensor(dstl, pF[:].rearrange(
                            "p (i o) -> p i o", i=4), 1.0, dsth,
                            OP.mult, OP.subtract).then_inc(s_dv, 1)
                    C.dv += 2
                if ot in (3, 7):
                    r_ = ot // 4
                    @on("sync")
                    def _(e, r_=r_, need=C.dv):
                        e.wait_ge(s_dv, need)
                        tb = tstage.bitcast(BF16)
                        for h in range(2):
                            for cc in range(8):
                                e.dma_start(
                                    tpt_d[cc][:, 1024 * h + 512 * r_:
                                              1024 * h + 512 * (r_ + 1)],
                                    tb.rearrange("p (h cc o) -> p h cc o",
                                                 h=2, cc=8)[:, h, cc, :]) \
                                    .then_inc(s_ld, 16)
                    C.ld += 16 * 16
                    prep_spill_ld[r_] = C.ld
            tstage_dv = C.dv
            tpt_ld = C.ld
            ckpt("prep")

            mk = {
                "xtm_free_dv": 0,
                "pf_free_dv": tstage_dv,
                "banks_free_dv": tstage_dv,
                "banks_free_ac": 0,
                "ebuf0_ok_te": 0,
                "tstage_spill_ld": tpt_ld,
                "ztmp_last_gp": 0,
                "pav_seed_dv": tstage_dv,
                "aug_free_te": 0,
                "vp_ev_dv": {},
                "rbc_read_dv": {},
                "trev_dv": {},
            }

            for u in range(UNITS):
                U0 = (u == 0)
                mk["vp_ev_dv"] = {}
                mk["rbc_read_dv"] = {}

                @on("sync")
                def _(e, u=u, need=mk["xtm_free_dv"]):
                    if need:
                        e.wait_ge(s_dv, need)
                    for st in range(4):
                        e.dma_start(x_tm[:, 1024 * st:1024 * (st + 1)],
                                    x_d[512 * u + 128 * st:512 * u + 128 * (st + 1), :]) \
                            .then_inc(s_ld, 16)
                C.ld += 64
                x_ld = C.ld

                for cc in range(8):
                    trb = pF if cc % 2 == 0 else pE
                    @on("tensor")
                    def _(e, cc=cc, trb=trb, x_ld=x_ld,
                          pf=mk["pf_free_dv"] if cc < 2 else 0,
                          pv=mk["trev_dv"].get(cc - 2, 0), init=init_ld):
                        e.wait_ge(s_ld, max(x_ld, init))
                        if pf:
                            e.wait_ge(s_dv, pf)
                        if pv:
                            e.wait_ge(s_dv, pv)
                        for st in range(4):
                            mmi = e.matmul(trb[:, 128 * st:128 * (st + 1)],
                                           x_tm[:, 1024 * st + 128 * cc:
                                                1024 * st + 128 * (cc + 1)],
                                           i128f[:], start=True, stop=True,
                                           is_transpose=True)
                            if st == 3:
                                mmi.then_inc(s_te, 1)
                    C.te += 1
                    tr = C.te
                    @on("vector")
                    def _(e, cc=cc, trb=trb, tr=tr, ok_te=mk["ebuf0_ok_te"]):
                        e.wait_ge(s_te, max(tr, ok_te))
                        e.tensor_copy(xT[:, 512 * cc:512 * (cc + 1)], trb[:]) \
                            .then_inc(s_dv, 1)
                    C.dv += 1
                    mk["trev_dv"][cc] = C.dv
                    mk["pf_free_dv"] = C.dv
                xT_dv = C.dv
                if U0:
                    tap("xT", xT)
                ckpt("xT", u)

                mm_done_of = {}
                evac_dv_of = {}
                evac_ac_of = {}
                for ot in range(24):
                    p_, seg = divmod(ot, 3)
                    r = ot % 4
                    quad = QUAD[r]
                    half = ot % 2
                    wslice, wt_ld = wst_load(half,
                                             wqkv_d[ot].rearrange("p cc o -> p (cc o)"))
                    @on("tensor")
                    def _(e, ot=ot, quad=quad, wslice=wslice, wt_ld=wt_ld,
                          xT_dv=xT_dv, pdv=evac_dv_of.get(ot - 4, 0),
                          pac=evac_ac_of.get(ot - 4, 0),
                          bdv=mk["banks_free_dv"], bac=mk["banks_free_ac"]):
                        e.wait_ge(s_ld, wt_ld)
                        e.wait_ge(s_dv, max(xT_dv, pdv, bdv))
                        if max(pac, bac):
                            e.wait_ge(s_ac, max(pac, bac))
                        for cc in range(8):
                            mmi = e.matmul(quad, wslice[:, 128 * cc:128 * (cc + 1)],
                                           xT[:, 512 * cc:512 * (cc + 1)],
                                           start=(cc == 0), stop=(cc == 7),
                                           skip_group_check=True)
                            if cc == 7:
                                mmi.then_inc(s_te, 1)
                    C.te += 1
                    mm_done_of[ot] = C.te
                    wst_user[half] = C.te

                    segbuf = [qb, kb, vb][seg]
                    for Hh in range(2):
                        dst = segbuf[64 * Hh:64 * (Hh + 1), 512 * p_:512 * (p_ + 1)]
                        src = quad[64 * Hh:64 * (Hh + 1), :]
                        bcol = bqc[64 * Hh:64 * (Hh + 1), ot:ot + 1]
                        if r < 2:
                            @on("vector")
                            def _(e, dst=dst, src=src, bcol=bcol, mm=mm_done_of[ot]):
                                e.wait_ge(s_te, mm)
                                e.tensor_scalar(dst, src, bcol, None, OP.add) \
                                    .then_inc(s_dv, 1)
                            C.dv += 1
                            evac_dv_of[ot] = C.dv
                        else:
                            @on("scalar")
                            def _(e, dst=dst, src=src, bcol=bcol, mm=mm_done_of[ot]):
                                e.wait_ge(s_te, mm)
                                e.activation(dst, src, AF.Identity, bias=bcol,
                                             scale=1.0).then_inc(s_ac, 1)
                            C.ac += 1
                            evac_ac_of[ot] = C.ac
                qkv_dv = C.dv
                qkv_ac = C.ac
                if U0:
                    tap("qbuf", qb[:])
                    tap("kbuf", kb[:])
                    tap("vbuf", vb[:])
                ckpt("qkv", u)

                @on("vector")
                def _(e, dv=qkv_dv, ac=qkv_ac):
                    e.wait_ge(s_dv, dv)
                    e.wait_ge(s_ac, ac)
                    e.tensor_copy(zb[:, 0:2048].bitcast(BF16), qb[:]) \
                        .then_inc(s_dv, 1)
                    e.scalar_tensor_tensor(zb[:, 2048:4096].bitcast(BF16),
                                           qb[:], 1.0,
                                           zb[:, 0:2048].bitcast(BF16),
                                           OP.mult, OP.subtract) \
                        .then_inc(s_dv, 1)
                C.dv += 2
                qsplit_dv = C.dv
                @on("gpsimd")
                def _(e, dv=qsplit_dv, prev_use=mk["aug_free_te"]):
                    if prev_use:
                        e.wait_ge(s_te, prev_use)
                    e.wait_ge(s_dv, dv)
                    e.memset(QAlo[64:128, :], 0.0).then_inc(s_gc, 1)
                    for half, dstb in ((0, QAhi[0:64, :]), (1, QAlo[0:64, :]),
                                       (0, QAhi[64:128, :])):
                        srcP = zb[:, 2048 * half:2048 * (half + 1)].bitcast(BF16)
                        d3 = dstb.rearrange("r (p s) -> r p s", s=512)
                        e.dma_start(d3[:, 0::2, :],
                                    srcP[0:64, :].rearrange("r (p s) -> r p s",
                                                            s=512)) \
                            .then_inc(s_gp, 16)
                        e.dma_start(d3[:, 1::2, :],
                                    srcP[64:128, :].rearrange("r (p s) -> r p s",
                                                              s=512)) \
                            .then_inc(s_gp, 16)
                C.gp += 96
                C.gc += 1
                qshuf_gp = C.gp
                @on("vector")
                def _(e, gp=qshuf_gp):
                    e.wait_ge(s_gp, gp)
                    e.tensor_copy(zb[:, 0:2048].bitcast(BF16), kb[:]) \
                        .then_inc(s_dv, 1)
                    e.scalar_tensor_tensor(zb[:, 2048:4096].bitcast(BF16),
                                           kb[:], 1.0,
                                           zb[:, 0:2048].bitcast(BF16),
                                           OP.mult, OP.subtract) \
                        .then_inc(s_dv, 1)
                C.dv += 2
                ksplit_dv = C.dv
                @on("gpsimd")
                def _(e, dv=ksplit_dv, tp=mk["tstage_spill_ld"]):
                    e.wait_ge(s_dv, dv)
                    e.wait_ge(s_ld, tp)
                    for half, dstb in ((0, KAhi[0:64, :]), (1, KAlo[64:128, :]),
                                       (0, KAlo[0:64, :])):
                        srcP = zb[:, 2048 * half:2048 * (half + 1)].bitcast(BF16)
                        d3 = dstb.rearrange("r (p s) -> r p s", s=512)
                        e.dma_start(d3[:, 0::2, :],
                                    srcP[0:64, :].rearrange("r (p s) -> r p s",
                                                            s=512)) \
                            .then_inc(s_gp, 16)
                        e.dma_start(d3[:, 1::2, :],
                                    srcP[64:128, :].rearrange("r (p s) -> r p s",
                                                              s=512)) \
                            .then_inc(s_gp, 16)
                    e.memset(KAhi[64:128, :], 0.0).then_inc(s_gc, 1)
                    e.memset(KAhi[64:66, :], 1.0).then_inc(s_gc, 1)
                C.gp += 96
                C.gc += 2
                shuf_gp = C.gp
                shuf_gc = C.gc

                avprologue = []
                for grp in range(16):
                    j_g = grp
                    Hk_g = j_g % 2
                    bank = pE if grp % 2 == 0 else pF
                    @on("tensor")
                    def _(e, grp=grp, bank=bank, dv=qkv_dv, ac=qkv_ac,
                          pf=mk["pf_free_dv"], pv=mk["vp_ev_dv"].get(grp - 2, 0)):
                        if grp == 0:
                            e.wait_ge(s_dv, max(dv, pf))
                            e.wait_ge(s_ac, ac)
                        if pv:
                            e.wait_ge(s_dv, pv)
                        for i in range(4):
                            tau = grp * 4 + i
                            j = tau // 4
                            w_ = tau % 4
                            Hk = j % 2
                            vt = vb[64 * Hk:64 * Hk + 64,
                                    512 * (j // 2) + 128 * w_:
                                    512 * (j // 2) + 128 * (w_ + 1)]
                            ident = i64d[64 * Hk:64 * Hk + 64, :]
                            mmi = e.matmul(bank[:, 64 * i:64 * (i + 1)], vt,
                                           ident, start=True, stop=True,
                                           is_transpose=True)
                            if i == 3:
                                mmi.then_inc(s_te, 1)
                    C.te += 1
                    tr = C.te
                    @on("vector")
                    def _(e, grp=grp, bank=bank, tr=tr, og=ones_gc):
                        e.wait_ge(s_te, tr)
                        e.wait_ge(s_gc, og)
                        dst = vp[:].rearrange("p (t c) -> p t c", c=65) \
                            [:, grp * 4:(grp + 1) * 4, 0:64]
                        dstl = vplo[:].rearrange("p (t c) -> p t c", c=65) \
                            [:, grp * 4:(grp + 1) * 4, 0:64]
                        src = bank[:, 0:256].rearrange("p (t c) -> p t c", c=64)
                        e.tensor_copy(dst, src).then_inc(s_dv, 1)
                        e.scalar_tensor_tensor(dstl, src, 1.0, dst,
                                               OP.mult, OP.subtract) \
                            .then_inc(s_dv, 1)
                    C.dv += 2
                    mk["vp_ev_dv"][grp] = C.dv
                vp_dv = C.dv
                mk["pf_free_dv"] = vp_dv
                if U0:
                    tap("vprime", vp[:])
                ckpt("vprime", u)

                scmB = {}
                elo_dv = {}
                elo_dvv = {}
                expB = {}
                avB = {}
                augdv = {}
                rank1te = {}
                sumsdv = {}
                zdv = {-2: mk["pav_seed_dv"], -1: mk["pav_seed_dv"]}
                zsh_carry = mk["ztmp_last_gp"]
                zsh = {}
                amm_te = {}
                ared_dv = {("slot", 0): vp_dv, ("slot", 1): vp_dv,
                           ("slot", 2): vp_dv}
                slot_last_te = {1: 0, 2: 0}
                slot_last_dv = {1: 0, 2: 0}
                ABANKS = {0: pE, 1: pC, 2: pD}

                def emit_A_mm(sigma, i, sl=0, xte=0, xdv=0):
                    bank = ABANKS[sl]
                    prev = ared_dv.get(("slot", sl), 0)
                    @on("tensor")
                    def _(e, sigma=sigma, i=i, bank=bank, prev=prev,
                          gp=shuf_gp, gc=shuf_gc, xte=xte, xdv=xdv):
                        if sigma == 0 and i == 0:
                            e.wait_ge(s_gp, gp)
                            e.wait_ge(s_gc, gc)
                        if xte:
                            e.wait_ge(s_te, xte)
                        if max(prev, xdv):
                            e.wait_ge(s_dv, max(prev, xdv))
                        e.matmul(bank[:, 0:512],
                                 QAhi[0:128, 128 * sigma:128 * (sigma + 1)],
                                 KAhi[0:128, 512 * i:512 * (i + 1)],
                                 start=True, stop=True).then_inc(s_te, 1)
                    C.te += 1
                    amm_te[(sigma, i)] = C.te
                    if sl:
                        slot_last_te[sl] = C.te
                    @on("vector")
                    def _(e, sigma=sigma, i=i, bank=bank, mm=amm_te[(sigma, i)]):
                        e.wait_ge(s_te, mm)
                        e.tensor_reduce(macc[:, i:i + 1], bank[:, 0:512], AX,
                                        OP.max).then_inc(s_dv, 1)
                    C.dv += 1
                    ared_dv[(sigma, i)] = C.dv
                    ared_dv[("slot", sl)] = C.dv
                    if sl:
                        slot_last_dv[sl] = C.dv

                def emit_A_merge(sigma):
                    dve_fence()
                    @on("vector")
                    def _(e, sigma=sigma):
                        e.tensor_reduce(negm[:, 0:1], macc[:], AX, OP.max) \
                            .then_inc(s_dv, 1)
                    C.dv += 1
                    dve_fence()
                    @on("vector")
                    def _(e, sigma=sigma):
                        e.tensor_scalar(negm[:, 0:1], negm[:, 0:1], 4.0, -1.0,
                                        OP.add, OP.mult).then_inc(s_dv, 1)
                    C.dv += 1
                    dve_fence()
                    kc = sigma % 4
                    @on("vector")
                    def _(e, kc=kc):
                        e.tensor_copy(negp[:, 2 * kc:2 * kc + 1], negm[:, 0:1]) \
                            .then_inc(s_dv, 1)
                    C.dv += 1
                    dve_fence()
                    @on("vector")
                    def _(e, kc=kc):
                        e.scalar_tensor_tensor(negp[:, 2 * kc + 1:2 * kc + 2],
                                               negm[:, 0:1],
                                               1.0, negp[:, 2 * kc:2 * kc + 1],
                                               OP.mult, OP.subtract) \
                            .then_inc(s_dv, 1)
                    C.dv += 1
                    dve_fence()
                    return C.dv

                def emit_aug(b):
                    evs = []
                    for k_ in range(4):
                        sigma = 4 * b + k_
                        mg = amerge[sigma]
                        @on("tensor")
                        def _(e, k_=k_, mg=mg, bar=C.te):
                            e.wait_ge(s_te, bar)
                            e.wait_ge(s_dv, mg)
                            e.matmul(pFb[0:2, 256 * k_:256 * k_ + 128],
                                     negp[:, 2 * k_:2 * k_ + 2], i128b[:],
                                     start=True, stop=True,
                                     is_transpose=True, skip_group_check=True) \
                                .then_inc(s_te, 1)
                        C.te += 1
                        evs.append(C.te)
                    @on("vector")
                    def _(e, b=b, evs=tuple(evs)):
                        e.wait_ge(s_te, evs[-1])
                        for k_ in range(4):
                            sigma = 4 * b + k_
                            e.tensor_copy(QAlo[64:66,
                                               128 * sigma:128 * (sigma + 1)],
                                          pFb[0:2, 256 * k_:256 * k_ + 128]) \
                                .then_inc(s_dv, 1)
                    C.dv += 4
                    augdv[b] = C.dv

                amerge = {}

                def drain_A(n, sl=0, xte=0, xdv=0):
                    for _ in range(n):
                        if not aqueue:
                            return
                        kind, a1, a2 = aqueue.pop(0)
                        if kind == "mm":
                            emit_A_mm(a1, a2, sl=sl, xte=xte, xdv=xdv)
                            xte = xdv = 0
                        elif kind == "mrg":
                            amerge[a1] = emit_A_merge(a1)
                        else:
                            emit_aug(a1)

                aqueue = []

                def queue_A_for_sb(b):
                    if b > 15:
                        return
                    for k_ in range(4):
                        sigma = 4 * b + k_
                        for i in range(16):
                            aqueue.append(("mm", sigma, i))
                        aqueue.append(("mrg", sigma, 0))
                    aqueue.append(("aug", b, 0))

                queue_A_for_sb(0)
                rot = 0
                while aqueue:
                    kind = aqueue[0][0]
                    drain_A(1, sl=(rot % 3) if kind == "mm" else 0)
                    if kind == "mm":
                        rot += 1
                queue_A_for_sb(1)
                queue_A_for_sb(2)
                queue_A_for_sb(3)

                SLOTB = [pA[:, 0:512], pA[:, 512:1024],
                         pB[:, 0:512], pB[:, 512:1024]]
                def emit_av3(b, tau, pav):
                    sl_b = 1 if b % 2 == 0 else 2
                    bx_te = slot_last_te[sl_b] if tau == 0 else 0
                    bx_dv = slot_last_dv[sl_b] if tau == 0 else 0
                    @on("tensor")
                    def _(e, b=b, tau=tau, pav=pav, xa=expB[(b, tau)],
                          lo=elo_dv[(b, tau)], lov=elo_dvv[(b, tau)],
                          bx_te=bx_te, bx_dv=bx_dv):
                        if bx_te:
                            e.wait_ge(s_te, bx_te)
                        e.wait_ge(s_dv, max(bx_dv, lov))
                        e.wait_ge(s_ac, xa)
                        e.wait_ge(s_gc, lo)
                        sc_ = 512 * (tau % 4)
                        e.matmul(pav[0:65, 0:512],
                                 vp[:, 65 * tau:65 * tau + 65],
                                 etslh[:, sc_:sc_ + 512],
                                 start=(tau == 0), stop=False,
                                 skip_group_check=True)
                        e.matmul(pav[0:65, 0:512],
                                 vp[:, 65 * tau:65 * tau + 65],
                                 etsll[:, sc_:sc_ + 512],
                                 start=False, stop=False,
                                 skip_group_check=True)
                        e.matmul(pav[0:65, 0:512],
                                 vplo[:, 65 * tau:65 * tau + 65],
                                 etslh[:, sc_:sc_ + 512],
                                 start=False, stop=(tau == 63),
                                 skip_group_check=True).then_inc(s_te, 1)
                    C.te += 1
                    avB[(b, tau)] = C.te

                for b in range(16):
                    pav = pC if b % 2 == 0 else pD
                    for tau in range(64):
                        slot = SLOTB[tau % 4]
                        slot_free = expB.get((b, tau - 4), 0) if tau >= 4 \
                            else (expB.get((b - 1, 60 + tau), 0) if b > 0 else 0)
                        @on("tensor")
                        def _(e, b=b, tau=tau, slot=slot, sf=slot_free,
                              aug=augdv[b], pvf=zdv[b - 2] if tau == 0 else 0):
                            if tau == 0:
                                e.wait_ge(s_dv, aug)
                            if pvf:
                                e.wait_ge(s_dv, pvf)
                            if sf:
                                e.wait_ge(s_ac, sf)
                            qs = QAhi[0:128, 512 * b:512 * (b + 1)]
                            ql = QAlo[0:128, 512 * b:512 * (b + 1)]
                            kh = KAhi[0:128, 128 * tau:128 * (tau + 1)]
                            kl = KAlo[0:128, 128 * tau:128 * (tau + 1)]
                            e.matmul(slot, kl, qs, start=True, stop=False)
                            e.matmul(slot, kh, ql, start=False, stop=True,
                                     skip_group_check=True).then_inc(s_te, 1)
                        C.te += 1
                        scmB[(b, tau)] = C.te
                        ets_need = avB.get((b, tau - 4), 0) if tau >= 4 \
                            else (avB.get((b - 1, 60 + tau), 0) if b > 0 else 0)
                        loev = elo_dv.get((b, tau - 4), 0) if tau >= 4 \
                            else (elo_dv.get((b - 1, 60 + tau), 0)
                                  if b > 0 else 0)
                        loevv = elo_dvv.get((b, tau - 4), 0) if tau >= 4 \
                            else (elo_dvv.get((b - 1, 60 + tau), 0)
                                  if b > 0 else 0)
                        @on("scalar")
                        def _(e, b=b, tau=tau, slot=slot, mm=scmB[(b, tau)],
                              ets=ets_need, loev=loev, loevv=loevv):
                            e.wait_ge(s_te, max(mm, ets))
                            if loev:
                                e.wait_ge(s_gc, loev)
                            if loevv:
                                e.wait_ge(s_dv, loevv)
                            sc_ = 512 * (tau % 4)
                            e.activation(etsl[:, sc_:sc_ + 512], slot, AF.Exp,
                                         bias=0.0, scale=0.125) \
                                .then_inc(s_ac, 1)
                            e.activation(etslh[:, sc_:sc_ + 512],
                                         etsl[:, sc_:sc_ + 512], AF.Identity,
                                         bias=0.0, scale=1.0).then_inc(s_ac, 1)
                        C.ac += 2
                        expB[(b, tau)] = C.ac
                        @on("gpsimd")
                        def _(e, b=b, tau=tau, xa=expB[(b, tau)]):
                            e.wait_ge(s_ac, xa)
                            sc_ = 512 * (tau % 4)
                            e.tensor_tensor(etsll[:, sc_ + 128:sc_ + 512],
                                            etsl[:, sc_ + 128:sc_ + 512],
                                            etslh[:, sc_ + 128:sc_ + 512],
                                            OP.subtract).then_inc(s_gc, 1)
                        C.gc += 1
                        elo_dv[(b, tau)] = C.gc
                        @on("vector")
                        def _(e, b=b, tau=tau, xa=expB[(b, tau)]):
                            e.wait_ge(s_ac, xa)
                            sc_ = 512 * (tau % 4)
                            e.tensor_tensor(etsll[:, sc_:sc_ + 128],
                                            etsl[:, sc_:sc_ + 128],
                                            etslh[:, sc_:sc_ + 128],
                                            OP.subtract).then_inc(s_dv, 1)
                        C.dv += 1
                        elo_dvv[(b, tau)] = C.dv
                        if tau >= 2:
                            emit_av3(b, tau - 2, pav)
                        drain_A(1)
                        if tau in (52, 54, 56, 58, 60):
                            alt_sl = 2 if b % 2 == 0 else 1
                            drain_A(1, sl=alt_sl,
                                    xte=avB.get((b - 1, 63), 0),
                                    xdv=zdv.get(b - 1, 0))
                    emit_av3(b, 62, pav)
                    emit_av3(b, 63, pav)
                    @on("vector")
                    def _(e, b=b, av=avB[(b, 63)], pav=pav):
                        e.wait_ge(s_te, av)
                        e.tensor_copy(srow[64:65, 0:512], pav[64:65, 0:512]) \
                            .then_inc(s_dv, 1)
                    C.dv += 1
                    sumsdv[b] = C.dv
                    @on("tensor")
                    def _(e, b=b, sd=sumsdv[b], bar=C.te,
                          rb_free=mk["rbc_read_dv"].get(b - 1, 0)):
                        e.wait_ge(s_te, bar)
                        e.wait_ge(s_dv, max(sd, rb_free))
                        e.matmul(pF[0:64, 0:512], onesb[64:65, 0:64],
                                 srow[64:65, 0:512], start=True, stop=True) \
                            .then_inc(s_te, 1)
                    C.te += 1
                    rank1te[b] = C.te
                    @on("vector")
                    def _(e, b=b, r1=rank1te[b]):
                        e.wait_ge(s_te, r1)
                        e.reciprocal(rbc[:, 0:512], pF[0:64, 0:512]) \
                            .then_inc(s_dv, 1)
                    C.dv += 1
                    mk["rbc_read_dv"][b] = C.dv
                    dve_fence()
                    if b % 2 == 0:
                        @on("vector")
                        def _(e, b=b, pav=pav):
                            e.tensor_tensor(zb[0:64, 512 * (b // 2):
                                               512 * (b // 2 + 1)],
                                            pav[0:64, 0:512], rbc[:, 0:512],
                                            OP.mult).then_inc(s_dv, 1)
                        C.dv += 1
                        zdv[b] = C.dv
                        mk["rbc_read_dv"][b] = C.dv
                    else:
                        pp = 512 * ((b // 2) % 2)
                        @on("vector")
                        def _(e, b=b, pav=pav, pp=pp,
                              ztf=zsh.get(b - 2, zsh_carry)):
                            if ztf:
                                e.wait_ge(s_gp, ztf)
                            e.tensor_tensor(ztmp[:, pp:pp + 512],
                                            pav[0:64, 0:512], rbc[:, 0:512],
                                            OP.mult).then_inc(s_dv, 1)
                        C.dv += 1
                        zdv[b] = C.dv
                        mk["rbc_read_dv"][b] = C.dv
                        @on("gpsimd")
                        def _(e, b=b, pp=pp, zz=zdv[b]):
                            e.wait_ge(s_dv, zz)
                            e.dma_start(zb[64:128, 512 * (b // 2):
                                           512 * (b // 2 + 1)],
                                        ztmp[:, pp:pp + 512]) \
                                .then_inc(s_gp, 16)
                        C.gp += 16
                        zsh[b] = C.gp
                    queue_A_for_sb(b + 4)
                attn_dv = C.dv
                attn_gp = C.gp
                attn_ac = expB[(15, 63)]
                mk["ztmp_last_gp"] = zsh[15]
                mk["aug_free_te"] = scmB[(15, 63)]
                if U0:
                    tap("zbuf", zb[:])
                ckpt("attn", u)

                op_mm_of = {}
                for cc in range(8):
                    half = cc % 2
                    wslice, wt_ld = wst_load(half, wot_d[128 * cc:128 * (cc + 1), :])
                    @on("tensor")
                    def _(e, cc=cc, wslice=wslice, wt_ld=wt_ld, attn=attn_dv,
                          agp=attn_gp, aac=attn_ac):
                        e.wait_ge(s_ld, wt_ld)
                        if cc == 0:
                            e.wait_ge(s_dv, attn)
                            e.wait_ge(s_gp, agp)
                            e.wait_ge(s_ac, aac)
                        for b8 in range(8):
                            st, ob = divmod(b8, 2)
                            mmi = e.matmul(BANK8[b8],
                                           zb[:, 512 * cc + 128 * st:
                                              512 * cc + 128 * (st + 1)],
                                           wslice[:, 512 * ob:512 * (ob + 1)],
                                           start=(cc == 0), stop=False,
                                           skip_group_check=True)
                            if b8 == 7:
                                mmi.then_inc(s_te, 1)
                    C.te += 1
                    op_mm_of[cc] = C.te
                    wst_user[half] = C.te
                @on("tensor")
                def _(e, og=ones_gc):
                    for b8 in range(8):
                        st, ob = divmod(b8, 2)
                        mmi = e.matmul(BANK8[b8], onesb[0:1, 0:128],
                                       bor[0:1, 512 * ob:512 * (ob + 1)],
                                       start=False, stop=True,
                                       skip_group_check=True)
                        if b8 == 7:
                            mmi.then_inc(s_te, 1)
                C.te += 1
                op_te = C.te

                for b8 in range(8):
                    st, ob = divmod(b8, 2)
                    col = 1024 * st + 512 * ob
                    @on("vector")
                    def _(e, b8=b8, col=col, mm=op_te):
                        e.wait_ge(s_te, mm)
                        e.scalar_tensor_tensor(zb[:, col:col + 512], BANK8[b8],
                                               1.0, x_tm[:, col:col + 512],
                                               OP.mult, OP.add).then_inc(s_dv, 1)
                    C.dv += 1
                opev_dv = C.dv
                ckpt("outproj", u)
                if U0:
                    tap("resid1", zb[:])

                ln1_dv = opev_dv
                for st in range(4):
                    ln1_dv = emit_ln(zb[:, 1024 * st:1024 * (st + 1)],
                                     y1[:, 1024 * st:1024 * (st + 1)], ln1_dv)
                if U0:
                    tap("y1", y1[:])

                for cc in range(8):
                    trb = pF if cc % 2 == 0 else pE
                    @on("tensor")
                    def _(e, cc=cc, trb=trb, ln1=ln1_dv,
                          pf=mk["pf_free_dv"] if cc < 2 else 0,
                          pv=mk["trev_dv"].get(("y", cc - 2), 0)):
                        e.wait_ge(s_dv, max(ln1, pf, pv))
                        for st in range(4):
                            mmi = e.matmul(trb[:, 128 * st:128 * (st + 1)],
                                           y1[:, 1024 * st + 128 * cc:
                                              1024 * st + 128 * (cc + 1)],
                                           i128f[:], start=True, stop=True,
                                           is_transpose=True)
                            if st == 3:
                                mmi.then_inc(s_te, 1)
                    C.te += 1
                    tr = C.te
                    @on("vector")
                    def _(e, cc=cc, trb=trb, tr=tr):
                        e.wait_ge(s_te, tr)
                        e.tensor_copy(QAhi[:, 512 * cc:512 * (cc + 1)], trb[:]) \
                            .then_inc(s_dv, 1)
                        e.scalar_tensor_tensor(
                            QAhi[:, 4096 + 512 * cc:4096 + 512 * (cc + 1)],
                            trb[:], 1.0, QAhi[:, 512 * cc:512 * (cc + 1)],
                            OP.mult, OP.subtract).then_inc(s_dv, 1)
                    C.dv += 2
                    mk["trev_dv"][("y", cc)] = C.dv
                    mk["pf_free_dv"] = C.dv
                y1T_dv = C.dv

                had_mm_of = {}
                had_ev_dv = {}
                had_ev_ac = {}
                for ht in range(8):
                    half = ht % 2
                    r = ht % 4
                    quad = QUAD[r]
                    wslice, wt_ld = wst_load(half,
                                             hh_d[ht].rearrange("p cc h -> p (cc h)"),
                                             bf=True)
                    wsb = wslice.bitcast(BF16)
                    @on("tensor")
                    def _(e, ht=ht, quad=quad, wsb=wsb, wt_ld=wt_ld,
                          ydv=y1T_dv, opev=opev_dv,
                          pdv=had_ev_dv.get(ht - 4, 0), pac=had_ev_ac.get(ht - 4, 0)):
                        e.wait_ge(s_ld, wt_ld)
                        e.wait_ge(s_dv, max(ydv, opev, pdv))
                        if pac:
                            e.wait_ge(s_ac, pac)
                        n = 0
                        for hl in range(2):
                            for cc in range(8):
                                n += 1
                                mmi = e.matmul(
                                    quad, wsb[:, 128 * cc:128 * (cc + 1)],
                                    QAhi[:, 4096 * hl + 512 * cc:
                                         4096 * hl + 512 * (cc + 1)],
                                    start=(hl == 0 and cc == 0),
                                    stop=(hl == 1 and cc == 7),
                                    skip_group_check=True)
                                if n == 16:
                                    mmi.then_inc(s_te, 1)
                    C.te += 1
                    had_mm_of[ht] = C.te
                    wst_user[half] = C.te
                    if r < 2:
                        @on("vector")
                        def _(e, ht=ht, quad=quad, mm=had_mm_of[ht]):
                            e.wait_ge(s_te, mm)
                            e.tensor_copy(xh_fm[:, 512 * ht:512 * (ht + 1)], quad) \
                                .then_inc(s_dv, 1)
                        C.dv += 1
                        had_ev_dv[ht] = C.dv
                    else:
                        @on("scalar")
                        def _(e, ht=ht, quad=quad, mm=had_mm_of[ht]):
                            e.wait_ge(s_te, mm)
                            e.activation(xh_fm[:, 512 * ht:512 * (ht + 1)], quad,
                                         AF.Identity, bias=0.0, scale=1.0) \
                                .then_inc(s_ac, 1)
                        C.ac += 1
                        had_ev_ac[ht] = C.ac
                had_dv = C.dv
                had_ac = C.ac
                ckpt("had", u)

                for st in range(4):
                    for hg in range(2):
                        idx = st * 2 + hg
                        trb = pF if idx % 2 == 0 else pE
                        @on("tensor")
                        def _(e, st=st, hg=hg, trb=trb, hdv=had_dv, hac=had_ac,
                              pf=mk["pf_free_dv"] if idx < 2 else 0,
                              pv=mk["trev_dv"].get(("xh", idx - 2), 0)):
                            e.wait_ge(s_dv, max(hdv, pf, pv))
                            e.wait_ge(s_ac, hac)
                            for i in range(4):
                                ht = 4 * hg + i
                                mmi = e.matmul(trb[:, 128 * i:128 * (i + 1)],
                                               xh_fm[:, 512 * ht + 128 * st:
                                                    512 * ht + 128 * (st + 1)],
                                               i128f[:], start=True, stop=True,
                                               is_transpose=True)
                                if i == 3:
                                    mmi.then_inc(s_te, 1)
                        C.te += 1
                        tr = C.te
                        @on("vector")
                        def _(e, st=st, hg=hg, trb=trb, tr=tr):
                            e.wait_ge(s_te, tr)
                            e.tensor_copy(xh_tm[:, 1024 * st + 512 * hg:
                                                1024 * st + 512 * (hg + 1)],
                                          trb[:]).then_inc(s_dv, 1)
                        C.dv += 1
                        mk["trev_dv"][("xh", idx)] = C.dv
                        mk["pf_free_dv"] = C.dv
                xh_dv = C.dv
                ckpt("xh", u)
                xh_tr_te = C.te
                if U0:
                    tap("xh_tm", xh_tm)

                qx_dv = 0
                for st in range(4):
                    t0 = 1024 * st
                    @on("vector")
                    def _(e, t0=t0, pre=xh_dv):
                        e.wait_ge(s_dv, pre)
                        e.tensor_reduce(mcol[:], xh_tm[:, t0:t0 + 1024], AX,
                                        OP.max, apply_absolute_value=True) \
                            .then_inc(s_dv, 1)
                    C.dv += 1
                    dve_fence()
                    @on("vector")
                    def _(e):
                        e.tensor_scalar(mcol[:], mcol[:], EPS, None, OP.max) \
                            .then_inc(s_dv, 1)
                    C.dv += 1
                    dve_fence()
                    @on("vector")
                    def _(e, st=st):
                        e.tensor_scalar(sxc[:, st:st + 1], mcol[:], 1.0 / 7.0,
                                        None, OP.mult).then_inc(s_dv, 1)
                        e.reciprocal(rq[:], mcol[:]).then_inc(s_dv, 1)
                    C.dv += 2
                    dve_fence()
                    @on("vector")
                    def _(e, okte=had_mm_of[7]):
                        e.wait_ge(s_te, okte)
                        e.tensor_scalar(rq[:], rq[:], 7.0, None, OP.mult) \
                            .then_inc(s_dv, 1)
                    C.dv += 1
                    dve_fence()
                    @on("vector")
                    def _(e, t0=t0):
                        e.tensor_scalar(wfa, xh_tm[:, t0:t0 + 1024], rq[:], None,
                                        OP.mult).then_inc(s_dv, 1)
                    C.dv += 1
                    vdv = C.dv
                    @on("scalar")
                    def _(e, t0=t0, vdv=vdv):
                        e.wait_ge(s_dv, vdv)
                        e.activation(zb[:, 3072:4096], wfa, AF.Abs,
                                     bias=0.0, scale=1.0).then_inc(s_ac, 1)
                    C.ac += 1
                    asdv = C.ac
                    @on("vector")
                    def _(e, t0=t0, asdv=asdv):
                        e.wait_ge(s_ac, asdv)
                        zb1 = zb[:, 1024:2048]
                        zb2 = zb[:, 2048:3072]
                        e.tensor_scalar(zb[:, 0:1024], wfa, 0.0, 2.0,
                                        OP.is_ge, OP.mult).then_inc(s_dv, 1)
                        e.tensor_scalar(zb[:, 0:1024], zb[:, 0:1024], 1.0,
                                        None, OP.subtract).then_inc(s_dv, 1)
                        e.tensor_scalar(zb1, zb[:, 3072:4096], 0.5, None,
                                        OP.is_gt).then_inc(s_dv, 1)
                        cur, other = zb1, zb2
                        for kk in range(1, 7):
                            cmp = OP.is_gt if kk % 2 == 0 else OP.is_ge
                            e.scalar_tensor_tensor(other, zb[:, 3072:4096],
                                                   kk + 0.5, cur,
                                                   cmp, OP.add).then_inc(s_dv, 1)
                            cur, other = other, cur
                        e.tensor_tensor(qx_tm[:, t0:t0 + 1024], cur,
                                        zb[:, 0:1024], OP.mult) \
                            .then_inc(s_dv, 1)
                    C.dv += 10
                    qx_dv = C.dv
                if U0:
                    tap("sxc", sxc[:])

                qx_tr_last_te = 0
                pEb = pE[:].bitcast(BF16)
                for ic in range(8):
                    trb = pFb if ic % 2 == 0 else pEb
                    @on("tensor")
                    def _(e, ic=ic, trb=trb, qx=qx_dv,
                          pf=mk["pf_free_dv"] if ic < 2 else 0,
                          pv=mk["trev_dv"].get(("qx", ic - 2), 0)):
                        e.wait_ge(s_dv, max(qx, pf, pv))
                        for st in range(4):
                            mmi = e.matmul(trb[:, 128 * st:128 * (st + 1)],
                                           qx_tm[:, 1024 * st + 128 * ic:
                                                 1024 * st + 128 * (ic + 1)],
                                           i128b[:], start=True, stop=True,
                                           is_transpose=True)
                            if st == 3:
                                mmi.then_inc(s_te, 1)
                    C.te += 1
                    tr = C.te
                    qx_tr_last_te = C.te
                    @on("vector")
                    def _(e, ic=ic, trb=trb, tr=tr):
                        e.wait_ge(s_te, tr)
                        e.tensor_copy(qxTb[:, 512 * ic:512 * (ic + 1)],
                                      trb[:, 0:512]).then_inc(s_dv, 1)
                    C.dv += 1
                    mk["trev_dv"][("qx", ic)] = C.dv
                    mk["pf_free_dv"] = C.dv
                qxT_dv = C.dv
                if U0:
                    tap("qxT", qxT)
                ckpt("qx", u)

                fmm_of = {}
                for cc in range(8):
                    half = cc % 2
                    wslice, wt_ld = wst_load(half, tpt_d[cc], bf=True)
                    wsb2 = wslice.bitcast(BF16)
                    @on("tensor")
                    def _(e, cc=cc, wsb2=wsb2, wt_ld=wt_ld, qxd=qxT_dv,
                          hdv=had_dv, hac=had_ac):
                        e.wait_ge(s_ld, wt_ld)
                        e.wait_ge(s_dv, max(qxd, hdv))
                        if cc == 0:
                            e.wait_ge(s_ac, hac)
                        n = 0
                        for hl in range(2):
                            for b8 in range(8):
                                st, ob = divmod(b8, 2)
                                n += 1
                                mmi = e.matmul(
                                    BANK8[b8],
                                    qxTb[:, 512 * cc + 128 * st:
                                         512 * cc + 128 * (st + 1)],
                                    wsb2[:, 1024 * hl + 512 * ob:
                                         1024 * hl + 512 * (ob + 1)],
                                    start=(cc == 0 and hl == 0),
                                    stop=(cc == 7 and hl == 1),
                                    skip_group_check=True)
                                if n == 16:
                                    mmi.then_inc(s_te, 1)
                    C.te += 1
                    fmm_of[cc] = C.te
                    wst_user[half] = C.te
                fmm_te = C.te

                for b8 in range(8):
                    st, ob = divmod(b8, 2)
                    col = 1024 * st + 512 * ob
                    @on("vector")
                    def _(e, b8=b8, st=st, col=col, mm=fmm_te):
                        e.wait_ge(s_te, mm)
                        e.scalar_tensor_tensor(x_tm[:, col:col + 512], BANK8[b8],
                                               sxc[:, st:st + 1],
                                               y1[:, col:col + 512],
                                               OP.mult, OP.add).then_inc(s_dv, 1)
                    C.dv += 1
                ffnev_dv = C.dv
                ckpt("ffnev", u)
                mk["banks_free_dv"] = ffnev_dv
                mk["banks_free_ac"] = had_ac
                mk["ebuf0_ok_te"] = qx_tr_last_te
                mk["pav_seed_dv"] = ffnev_dv
                if U0:
                    tap("resid2", x_tm[:])

                ln2_dv = ffnev_dv
                for st in range(4):
                    ln2_dv = emit_ln(x_tm[:, 1024 * st:1024 * (st + 1)], wfa,
                                     ln2_dv, pre_st=C.st)
                    @on("sync")
                    def _(e, u=u, st=st, need=ln2_dv):
                        e.wait_ge(s_dv, need)
                        e.dma_start(y_d[512 * u + 128 * st:512 * u + 128 * (st + 1), :],
                                    wfa).then_inc(s_st, 16)
                    C.st += 16
                mk["xtm_free_dv"] = ln2_dv
                ckpt("unit", u)

        try:
            emit_all()
        except _Stop:
            pass

        @on("sync")
        def _(e, st_tgt=C.st, ld_tgt=C.ld):
            e.wait_ge(s_st, st_tgt)
            e.wait_ge(s_ld, ld_tgt)
        @on("gpsimd")
        def _(e, gp_tgt=C.gp):
            e.wait_ge(s_gp, gp_tgt)

        block = ctx.enter_context(nc.Block())

        @block.sync
        def _(e):
            for fn in plan["sync"]:
                fn(e)

        @block.gpsimd
        def _(e):
            for fn in plan["gpsimd"]:
                fn(e)

        @block.tensor
        def _(e):
            for fn in plan["tensor"]:
                fn(e)

        @block.vector
        def _(e):
            for fn in plan["vector"]:
                fn(e)

        @block.scalar
        def _(e):
            for fn in plan["scalar"]:
                fn(e)

    return nc


def _hadamard(size):
    h = np.array([[1.0]], dtype=np.float64)
    while h.shape[0] < size:
        h = np.block([[h, h], [h, -h]])
    return (h / np.sqrt(size)).astype(np.float32)


def host_inputs(L0_attn_x, L0_attn_Wqkv, L0_attn_bqkv, L0_attn_Wo, L0_attn_bo,
                L0_ffn_weight):
    import ml_dtypes
    x = np.ascontiguousarray(np.asarray(L0_attn_x, np.float32)
                             .reshape(16384, 1024))
    Wqkv = np.asarray(L0_attn_Wqkv, np.float32)
    bqkv = np.asarray(L0_attn_bqkv, np.float32)
    Wo = np.asarray(L0_attn_Wo, np.float32)
    bo = np.asarray(L0_attn_bo, np.float32)
    Wf = np.ascontiguousarray(np.asarray(L0_ffn_weight, np.float32))

    idx = np.empty(3072, np.int64)
    for ot in range(24):
        p_, seg = divmod(ot, 3)
        for r_ in range(128):
            Hh, d = divmod(r_, 64)
            idx[ot * 128 + r_] = (2 * p_ + Hh) * 192 + seg * 64 + d
    Wperm = Wqkv[idx]
    bperm = np.ascontiguousarray(bqkv[idx])
    Wt = Wperm.T
    wqkv4 = np.ascontiguousarray(
        Wt.reshape(8, 128, 24, 128).transpose(2, 1, 0, 3))
    wot = np.ascontiguousarray(Wo.T)
    H = _hadamard(1024)
    hh4 = np.ascontiguousarray(
        H.reshape(8, 128, 8, 128).transpose(2, 1, 0, 3)).astype(
            ml_dtypes.bfloat16)
    i128f = np.eye(128, dtype=np.float32)
    i128b = np.eye(128).astype(ml_dtypes.bfloat16)
    i64d = np.vstack([np.eye(64), np.eye(64)]).astype(np.float32)

    shared = {
        "wqkv": wqkv4, "bqkv": bperm, "wot": wot,
        "bo": np.ascontiguousarray(bo.reshape(1, 1024)),
        "hh": hh4, "wf": Wf, "i128f": i128f, "i128b": i128b, "i64d": i64d,
    }
    in_maps = []
    for c in range(8):
        m = dict(shared)
        m["x"] = np.ascontiguousarray(x[2048 * c:2048 * (c + 1)])
        in_maps.append(m)
    return in_maps


def kernel(**inputs):
    in_maps = host_inputs(**inputs)
    nc = build_program(taps=DEBUG_TAPS)
    res = run_bass_kernel_spmd(nc, in_maps, core_ids=list(range(8)))
    y = np.concatenate([np.asarray(r["y"]) for r in res.results], axis=0)
    return y.reshape(2, 8192, 1024).astype(np.float32)


# version 26 (final)
# speedup vs baseline: 1.2324x; 1.2324x over previous
import numpy as np
import concourse.bass as bass
import concourse.mybir as mybir
from concourse.bass_utils import run_bass_kernel_spmd
from contextlib import ExitStack

F32 = mybir.dt.float32
BF16 = mybir.dt.bfloat16
AX = mybir.AxisListType.X
AF = mybir.ActivationFunctionType
OP = mybir.AluOpType

UNITS = 4
SLABW = 256
NSLAB = 32
EPS = 1e-5

DEBUG_TAPS = {}


def build_program(taps=None, stop_after=None):
    taps = dict(taps or {})
    nc = bass.Bass("TRN2", target_bir_lowering=False)

    x_d = nc.dram_tensor("x", [2048, 1024], F32, kind="ExternalInput").ap()
    wqkv_d = nc.dram_tensor("wqkv", [24, 128, 8, 128], F32, kind="ExternalInput").ap()
    bqkv_d = nc.dram_tensor("bqkv", [3072], F32, kind="ExternalInput").ap()
    wot_d = nc.dram_tensor("wot", [1024, 1024], F32, kind="ExternalInput").ap()
    bo_d = nc.dram_tensor("bo", [1, 1024], F32, kind="ExternalInput").ap()
    hh_d = nc.dram_tensor("hh", [8, 128, 8, 128], BF16, kind="ExternalInput").ap()
    wf_d = nc.dram_tensor("wf", [1024, 1024], F32, kind="ExternalInput").ap()
    i128f_d = nc.dram_tensor("i128f", [128, 128], F32, kind="ExternalInput").ap()
    i128b_d = nc.dram_tensor("i128b", [128, 128], BF16, kind="ExternalInput").ap()
    i64d_d = nc.dram_tensor("i64d", [128, 64], F32, kind="ExternalInput").ap()
    y_d = nc.dram_tensor("y", [2048, 1024], F32, kind="ExternalOutput").ap()
    tpt_d = nc.dram_tensor("tpt_scratch", [8, 128, 2048], BF16).ap()

    tap_d = {name: nc.dram_tensor(
                 "tap_" + name,
                 list(sh[0]) if isinstance(sh[0], (tuple, list)) else list(sh),
                 sh[1] if isinstance(sh[0], (tuple, list)) else F32,
                 kind="ExternalOutput").ap()
             for name, sh in taps.items()}

    ctx = ExitStack()
    with ctx:
        sbt = lambda n, s, d=F32: ctx.enter_context(nc.sbuf_tensor(n, s, d))
        pst = lambda n, s, d=F32: ctx.enter_context(nc.psum_tensor(n, s, d))
        mksem = lambda n: ctx.enter_context(nc.semaphore(n))

        x_tm = sbt("x_tm", [128, 4096])
        qb = sbt("qb", [128, 4096])
        kb = sbt("kb", [128, 4096])
        vb = sbt("vb", [128, 4096])
        vp = sbt("vp", [128, 64 * 65], BF16)
        vplo = sbt("vplo", [128, 64 * 65], BF16)
        QAhi = sbt("QAhi", [128, 8192], BF16)
        QAlo = sbt("QAlo", [128, 8192], BF16)
        KAhi = sbt("KAhi", [128, 8192], BF16)
        KAlo = sbt("KAlo", [128, 8192], BF16)
        etsl = sbt("etsl", [128, 2048])
        etslh = sbt("etslh", [128, 2048], BF16)
        etsll = sbt("etsll", [128, 2048], BF16)
        macc = sbt("macc", [128, 16])
        negm = sbt("negm", [128, 1])
        negmh = sbt("negmh", [128, 4], BF16)
        negp = sbt("negp", [128, 8], BF16)
        negml = sbt("negml", [128, 4], BF16)
        zb = sbt("zb", [128, 4096])
        wst = sbt("wst", [128, 2048])
        bqc = sbt("bqc", [128, 24])
        bor = sbt("bor", [1, 1024])
        onesb = sbt("onesb", [128, 128])
        i128f = sbt("i128f_s", [128, 128])
        i128b = sbt("i128b_s", [128, 128], BF16)
        i64d = sbt("i64d_s", [128, 64])
        negB = sbt("negB", [128, 1])
        srow = sbt("srow", [128, 512])
        rbc = sbt("rbc", [64, 512])
        ztmp = sbt("ztmp", [64, 1024])
        sc1 = sbt("sc1", [1, 1])
        sc2 = sbt("sc2", [1, 1])
        sc3 = sbt("sc3", [1, 1])
        mvst = sbt("mvst", [128, 12])
        mv = sbt("mv", [128, 2])
        rstd = sbt("rstd", [128, 1])
        mcol = sbt("mcol", [128, 1])
        sxc = sbt("sxc", [128, 4])
        rq = sbt("rq", [128, 1])

        y1 = qb[:, 0:4096]
        wfa = vb[:, 0:1024]
        wfs = vb[:, 1024:2048]
        qx_tm = vb[:, 1024:3072].bitcast(BF16)
        lad_a = vb[:, 3072:3584].bitcast(BF16)
        lad_b = vb[:, 3584:4096].bitcast(BF16)
        xT = QAhi[:].bitcast(F32)
        y1T = QAhi[:].bitcast(F32)
        xh_fm = QAlo[:].bitcast(F32)
        xh_tm = KAhi[:].bitcast(F32)
        qxT = KAlo[:].bitcast(F32)
        qxTb = KAlo[:, 0:4096]
        tstage = KAhi[:].bitcast(F32)

        pA = pst("pA", [128, 1024])
        pB = pst("pB", [128, 1024])
        pC = pst("pC", [128, 512])
        pD = pst("pD", [128, 512])
        pE = pst("pE", [128, 512])
        pF = pst("pF", [128, 512])
        BANK8 = [pA[:, 0:512], pA[:, 512:1024], pB[:, 0:512], pB[:, 512:1024],
                 pC[:, 0:512], pD[:, 0:512], pE[:, 0:512], pF[:, 0:512]]
        QUAD = [pA[:, 0:512], pA[:, 512:1024], pB[:, 0:512], pB[:, 512:1024]]
        pFb = pF[:].bitcast(BF16)

        s_ld = mksem("s_ld")
        s_gp = mksem("s_gp")
        s_te = mksem("s_te")
        s_dv = mksem("s_dv")
        s_ac = mksem("s_ac")
        s_gc = mksem("s_gc")
        s_st = mksem("s_st")

        class C:
            ld = 0; gp = 0; te = 0; dv = 0; ac = 0; gc = 0; st = 0

        plan = {k: [] for k in ("sync", "gpsimd", "tensor", "vector", "scalar")}

        def on(engine):
            def deco(fn):
                plan[engine].append(fn)
            return deco

        def dve_fence():
            @on("vector")
            def _(e, tgt=C.dv):
                e.wait_ge(s_dv, tgt)

        def tap(name, src):
            if name not in tap_d:
                return
            @on("sync")
            def _(e, src=src, dst=tap_d[name], dvv=C.dv, acv=C.ac, tev=C.te,
                  gpv=C.gp):
                e.wait_ge(s_dv, dvv)
                e.wait_ge(s_ac, acv)
                e.wait_ge(s_te, tev)
                if gpv:
                    e.wait_ge(s_gp, gpv)
                e.dma_start(dst, src).then_inc(s_st, 16)
            C.st += 16

        wst_user = {0: 0, 1: 0}

        def wst_load(half, src_ap, gate_dv=0, bf=False):
            wslice = wst[:, half * 1024:(half + 1) * 1024]
            if bf:
                nfree = 1
                for d_ in src_ap.shape[1:]:
                    nfree *= d_
                dst = wslice.bitcast(BF16)[:, 0:nfree]
            else:
                dst = wslice
            @on("sync")
            def _(e, dst=dst, src_ap=src_ap, need=wst_user[half],
                  gdv=gate_dv):
                if need:
                    e.wait_ge(s_te, need)
                if gdv:
                    e.wait_ge(s_dv, gdv)
                e.dma_start(dst, src_ap).then_inc(s_ld, 16)
            C.ld += 16
            return wslice, C.ld

        def emit_ln(src_ap, dst_ap, pre_dv, pre_ac=0, pre_st=0):
            @on("vector")
            def _(e, src_ap=src_ap, pre=pre_dv, pac=pre_ac):
                e.wait_ge(s_dv, pre)
                if pac:
                    e.wait_ge(s_ac, pac)
                e.bn_stats(mvst[:, 0:6], src_ap[:, 0:512]).then_inc(s_dv, 1)
                e.bn_stats(mvst[:, 6:12], src_ap[:, 512:1024]).then_inc(s_dv, 1)
            C.dv += 2
            dve_fence()
            @on("vector")
            def _(e):
                e.bn_aggr(mv[:], mvst[:]).then_inc(s_dv, 1)
            C.dv += 1
            dve_fence()
            @on("vector")
            def _(e):
                e.tensor_scalar(mcol[:], mv[:, 1:2], EPS, None, OP.add) \
                    .then_inc(s_dv, 1)
            C.dv += 1
            f = C.dv
            @on("scalar")
            def _(e, f=f):
                e.wait_ge(s_dv, f)
                e.activation(rstd[:], mcol[:], AF.Sqrt).then_inc(s_ac, 1)
            C.ac += 1
            fa = C.ac
            @on("vector")
            def _(e, fa=fa):
                e.wait_ge(s_ac, fa)
                e.reciprocal(mcol[:], rstd[:]).then_inc(s_dv, 1)
            C.dv += 1
            dve_fence()
            @on("vector")
            def _(e, src_ap=src_ap, dst_ap=dst_ap, pst_=pre_st):
                if pst_:
                    e.wait_ge(s_st, pst_)
                e.tensor_scalar(dst_ap, src_ap, mv[:, 0:1], mcol[:],
                                OP.subtract, OP.mult).then_inc(s_dv, 1)
            C.dv += 1
            return C.dv

        class _Stop(Exception):
            pass

        def ckpt(name, u=0):
            if stop_after is not None and name == stop_after and u == 0:
                raise _Stop()

        def emit_all():
            @on("sync")
            def _(e):
                e.dma_start(i128f[:], i128f_d).then_inc(s_ld, 16)
                e.dma_start(i128b[:], i128b_d).then_inc(s_ld, 16)
                e.dma_start(i64d[:], i64d_d).then_inc(s_ld, 16)
                e.dma_start(bor[:], bo_d).then_inc(s_ld, 16)
                e.dma_start(bqc[:], bqkv_d.rearrange("(t p) -> p t", p=128),
                            allow_slow_non_contiguous=True).then_inc(s_ld, 16)
            C.ld += 5 * 16
            init_ld = C.ld

            @on("gpsimd")
            def _(e):
                e.memset(onesb[:], 1.0).then_inc(s_gc, 1)
                e.memset(vp[:], 1.0).then_inc(s_gc, 1)
                e.memset(vplo[:], 0.0).then_inc(s_gc, 1)
            C.gc += 3
            ones_gc = C.gc

            prep_dv_of = {}
            prep_tr_te = {}
            prep_spill_ld = {}
            for ot in range(8):
                half = ot % 2
                wslice = wst[:, half * 1024:(half + 1) * 1024]
                @on("sync")
                def _(e, ot=ot, wslice=wslice, need=prep_dv_of.get(ot - 2, 0)):
                    if need:
                        e.wait_ge(s_dv, need)
                    e.dma_start(wslice, wf_d[128 * ot:128 * (ot + 1), :]) \
                        .then_inc(s_ld, 16)
                C.ld += 16
                ld_here = C.ld

                @on("scalar")
                def _(e, wslice=wslice, ld=ld_here,
                      trp=prep_tr_te.get(ot - 1, 0)):
                    e.wait_ge(s_ld, ld)
                    if trp:
                        e.wait_ge(s_te, trp)
                    e.activation(wfa, wslice, AF.Abs).then_inc(s_ac, 1)
                C.ac += 1
                abs_ac = C.ac

                @on("vector")
                def _(e, abs_ac=abs_ac):
                    e.wait_ge(s_ac, abs_ac)
                    e.tensor_reduce(rstd[:], wfa, AX, OP.add).then_inc(s_dv, 1)
                C.dv += 1
                dve_fence()
                @on("vector")
                def _(e):
                    e.tensor_scalar(rstd[:], rstd[:], 1.0 / 1024.0, EPS,
                                    OP.mult, OP.max).then_inc(s_dv, 1)
                C.dv += 1
                dve_fence()
                @on("vector")
                def _(e):
                    e.reciprocal(mcol[:], rstd[:]).then_inc(s_dv, 1)
                C.dv += 1
                dve_fence()
                @on("vector")
                def _(e, wslice=wslice):
                    e.tensor_scalar(wfs, wslice, mcol[:], None, OP.mult) \
                        .then_inc(s_dv, 1)
                C.dv += 1
                prep_dv_of[ot] = C.dv
                @on("vector")
                def _(e):
                    e.tensor_scalar(wfa, wfs, -0.5, None, OP.is_lt).then_inc(s_dv, 1)
                    e.scalar_tensor_tensor(wfs, wfs, 0.5, wfa, OP.is_gt,
                                           OP.subtract).then_inc(s_dv, 1)
                    e.tensor_scalar(wfa, wfs, rstd[:], None, OP.mult) \
                        .then_inc(s_dv, 1)
                C.dv += 3
                tp_dv = C.dv

                for g in range(2):
                    @on("tensor")
                    def _(e, g=g, tp_dv=tp_dv, init=init_ld, prior=C.dv):
                        e.wait_ge(s_ld, init)
                        e.wait_ge(s_dv, max(tp_dv, prior))
                        for i in range(4):
                            cc = g * 4 + i
                            mmi = e.matmul(pF[:, 128 * i:128 * (i + 1)],
                                           wfa[:, 128 * cc:128 * (cc + 1)], i128f[:],
                                           start=True, stop=True, is_transpose=True)
                            if i == 3:
                                mmi.then_inc(s_te, 1)
                    C.te += 1
                    tr_te = C.te
                    prep_tr_te[ot] = C.te
                    @on("vector")
                    def _(e, ot=ot, g=g, tr=tr_te,
                          sp=prep_spill_ld.get(ot // 4 - 1, 0)):
                        e.wait_ge(s_te, tr)
                        if sp:
                            e.wait_ge(s_ld, sp)
                        tb = tstage.bitcast(BF16)
                        dsth = tb.rearrange("p (h cc o) -> p h cc o", h=2, cc=8) \
                            [:, 0, g * 4:(g + 1) * 4,
                             128 * (ot % 4):128 * (ot % 4 + 1)]
                        dstl = tb.rearrange("p (h cc o) -> p h cc o", h=2, cc=8) \
                            [:, 1, g * 4:(g + 1) * 4,
                             128 * (ot % 4):128 * (ot % 4 + 1)]
                        src = pF[:].rearrange("p (i o) -> p i o", i=4)
                        e.tensor_copy(dsth, src).then_inc(s_dv, 1)
                        e.scalar_tensor_tensor(dstl, pF[:].rearrange(
                            "p (i o) -> p i o", i=4), 1.0, dsth,
                            OP.mult, OP.subtract).then_inc(s_dv, 1)
                    C.dv += 2
                if ot in (3, 7):
                    r_ = ot // 4
                    @on("sync")
                    def _(e, r_=r_, need=C.dv):
                        e.wait_ge(s_dv, need)
                        tb = tstage.bitcast(BF16)
                        for h in range(2):
                            for cc in range(8):
                                e.dma_start(
                                    tpt_d[cc][:, 1024 * h + 512 * r_:
                                              1024 * h + 512 * (r_ + 1)],
                                    tb.rearrange("p (h cc o) -> p h cc o",
                                                 h=2, cc=8)[:, h, cc, :]) \
                                    .then_inc(s_ld, 16)
                    C.ld += 16 * 16
                    prep_spill_ld[r_] = C.ld
            tstage_dv = C.dv
            tpt_ld = C.ld
            ckpt("prep")

            mk = {
                "xtm_free_dv": 0,
                "pf_free_dv": tstage_dv,
                "banks_free_dv": tstage_dv,
                "banks_free_ac": 0,
                "ebuf0_ok_te": 0,
                "tstage_spill_ld": tpt_ld,
                "ztmp_last_gp": 0,
                "pav_seed_dv": tstage_dv,
                "aug_free_te": 0,
                "vp_ev_dv": {},
                "rbc_read_dv": {},
                "trev_dv": {},
            }

            for u in range(UNITS):
                U0 = (u == 0)
                mk["vp_ev_dv"] = {}
                mk["rbc_read_dv"] = {}

                @on("sync")
                def _(e, u=u, need=mk["xtm_free_dv"]):
                    if need:
                        e.wait_ge(s_dv, need)
                    for st in range(4):
                        e.dma_start(x_tm[:, 1024 * st:1024 * (st + 1)],
                                    x_d[512 * u + 128 * st:512 * u + 128 * (st + 1), :]) \
                            .then_inc(s_ld, 16)
                C.ld += 64
                x_ld = C.ld

                for cc in range(8):
                    trb = pF if cc % 2 == 0 else pE
                    @on("tensor")
                    def _(e, cc=cc, trb=trb, x_ld=x_ld,
                          pf=mk["pf_free_dv"] if cc < 2 else 0,
                          pv=mk["trev_dv"].get(cc - 2, 0), init=init_ld):
                        e.wait_ge(s_ld, max(x_ld, init))
                        if pf:
                            e.wait_ge(s_dv, pf)
                        if pv:
                            e.wait_ge(s_dv, pv)
                        for st in range(4):
                            mmi = e.matmul(trb[:, 128 * st:128 * (st + 1)],
                                           x_tm[:, 1024 * st + 128 * cc:
                                                1024 * st + 128 * (cc + 1)],
                                           i128f[:], start=True, stop=True,
                                           is_transpose=True)
                            if st == 3:
                                mmi.then_inc(s_te, 1)
                    C.te += 1
                    tr = C.te
                    @on("vector")
                    def _(e, cc=cc, trb=trb, tr=tr, ok_te=mk["ebuf0_ok_te"]):
                        e.wait_ge(s_te, max(tr, ok_te))
                        e.tensor_copy(xT[:, 512 * cc:512 * (cc + 1)], trb[:]) \
                            .then_inc(s_dv, 1)
                    C.dv += 1
                    mk["trev_dv"][cc] = C.dv
                    mk["pf_free_dv"] = C.dv
                xT_dv = C.dv
                if U0:
                    tap("xT", xT)
                ckpt("xT", u)

                mm_done_of = {}
                evac_dv_of = {}
                evac_ac_of = {}
                for ot in range(24):
                    p_, seg = divmod(ot, 3)
                    r = ot % 4
                    quad = QUAD[r]
                    half = ot % 2
                    wslice, wt_ld = wst_load(half,
                                             wqkv_d[ot].rearrange("p cc o -> p (cc o)"))
                    @on("tensor")
                    def _(e, ot=ot, quad=quad, wslice=wslice, wt_ld=wt_ld,
                          xT_dv=xT_dv, pdv=evac_dv_of.get(ot - 4, 0),
                          pac=evac_ac_of.get(ot - 4, 0),
                          bdv=mk["banks_free_dv"], bac=mk["banks_free_ac"]):
                        e.wait_ge(s_ld, wt_ld)
                        e.wait_ge(s_dv, max(xT_dv, pdv, bdv))
                        if max(pac, bac):
                            e.wait_ge(s_ac, max(pac, bac))
                        for cc in range(8):
                            mmi = e.matmul(quad, wslice[:, 128 * cc:128 * (cc + 1)],
                                           xT[:, 512 * cc:512 * (cc + 1)],
                                           start=(cc == 0), stop=(cc == 7),
                                           skip_group_check=True)
                            if cc == 7:
                                mmi.then_inc(s_te, 1)
                    C.te += 1
                    mm_done_of[ot] = C.te
                    wst_user[half] = C.te

                    segbuf = [qb, kb, vb][seg]
                    for Hh in range(2):
                        dst = segbuf[64 * Hh:64 * (Hh + 1), 512 * p_:512 * (p_ + 1)]
                        src = quad[64 * Hh:64 * (Hh + 1), :]
                        bcol = bqc[64 * Hh:64 * (Hh + 1), ot:ot + 1]
                        if r < 2:
                            @on("vector")
                            def _(e, dst=dst, src=src, bcol=bcol, mm=mm_done_of[ot]):
                                e.wait_ge(s_te, mm)
                                e.tensor_scalar(dst, src, bcol, None, OP.add) \
                                    .then_inc(s_dv, 1)
                            C.dv += 1
                            evac_dv_of[ot] = C.dv
                        else:
                            @on("scalar")
                            def _(e, dst=dst, src=src, bcol=bcol, mm=mm_done_of[ot]):
                                e.wait_ge(s_te, mm)
                                e.activation(dst, src, AF.Identity, bias=bcol,
                                             scale=1.0).then_inc(s_ac, 1)
                            C.ac += 1
                            evac_ac_of[ot] = C.ac
                qkv_dv = C.dv
                qkv_ac = C.ac
                if U0:
                    tap("qbuf", qb[:])
                    tap("kbuf", kb[:])
                    tap("vbuf", vb[:])
                ckpt("qkv", u)

                @on("vector")
                def _(e, dv=qkv_dv, ac=qkv_ac):
                    e.wait_ge(s_dv, dv)
                    e.wait_ge(s_ac, ac)
                    e.tensor_copy(zb[:, 0:2048].bitcast(BF16), qb[:]) \
                        .then_inc(s_dv, 1)
                    e.scalar_tensor_tensor(zb[:, 2048:4096].bitcast(BF16),
                                           qb[:], 1.0,
                                           zb[:, 0:2048].bitcast(BF16),
                                           OP.mult, OP.subtract) \
                        .then_inc(s_dv, 1)
                C.dv += 2
                qsplit_dv = C.dv
                @on("gpsimd")
                def _(e, dv=qsplit_dv, prev_use=mk["aug_free_te"]):
                    if prev_use:
                        e.wait_ge(s_te, prev_use)
                    e.wait_ge(s_dv, dv)
                    e.memset(QAlo[64:128, :], 0.0).then_inc(s_gc, 1)
                    for half, dstb in ((0, QAhi[0:64, :]), (1, QAlo[0:64, :]),
                                       (0, QAhi[64:128, :])):
                        srcP = zb[:, 2048 * half:2048 * (half + 1)].bitcast(BF16)
                        d3 = dstb.rearrange("r (p s) -> r p s", s=512)
                        e.dma_start(d3[:, 0::2, :],
                                    srcP[0:64, :].rearrange("r (p s) -> r p s",
                                                            s=512)) \
                            .then_inc(s_gp, 16)
                        e.dma_start(d3[:, 1::2, :],
                                    srcP[64:128, :].rearrange("r (p s) -> r p s",
                                                              s=512)) \
                            .then_inc(s_gp, 16)
                C.gp += 96
                C.gc += 1
                qshuf_gp = C.gp
                @on("vector")
                def _(e, gp=qshuf_gp):
                    e.wait_ge(s_gp, gp)
                    e.tensor_copy(zb[:, 0:2048].bitcast(BF16), kb[:]) \
                        .then_inc(s_dv, 1)
                    e.scalar_tensor_tensor(zb[:, 2048:4096].bitcast(BF16),
                                           kb[:], 1.0,
                                           zb[:, 0:2048].bitcast(BF16),
                                           OP.mult, OP.subtract) \
                        .then_inc(s_dv, 1)
                C.dv += 2
                ksplit_dv = C.dv
                @on("gpsimd")
                def _(e, dv=ksplit_dv, tp=mk["tstage_spill_ld"]):
                    e.wait_ge(s_dv, dv)
                    e.wait_ge(s_ld, tp)
                    for half, dstb in ((0, KAhi[0:64, :]), (1, KAlo[64:128, :]),
                                       (0, KAlo[0:64, :])):
                        srcP = zb[:, 2048 * half:2048 * (half + 1)].bitcast(BF16)
                        d3 = dstb.rearrange("r (p s) -> r p s", s=512)
                        e.dma_start(d3[:, 0::2, :],
                                    srcP[0:64, :].rearrange("r (p s) -> r p s",
                                                            s=512)) \
                            .then_inc(s_gp, 16)
                        e.dma_start(d3[:, 1::2, :],
                                    srcP[64:128, :].rearrange("r (p s) -> r p s",
                                                              s=512)) \
                            .then_inc(s_gp, 16)
                    e.memset(KAhi[64:128, :], 0.0).then_inc(s_gc, 1)
                    e.memset(KAhi[64:66, :], 1.0).then_inc(s_gc, 1)
                C.gp += 96
                C.gc += 2
                shuf_gp = C.gp
                shuf_gc = C.gc

                avprologue = []
                for grp in range(16):
                    j_g = grp
                    Hk_g = j_g % 2
                    bank = pE if grp % 2 == 0 else pF
                    @on("tensor")
                    def _(e, grp=grp, bank=bank, dv=qkv_dv, ac=qkv_ac,
                          pf=mk["pf_free_dv"], pv=mk["vp_ev_dv"].get(grp - 2, 0)):
                        if grp == 0:
                            e.wait_ge(s_dv, max(dv, pf))
                            e.wait_ge(s_ac, ac)
                        if pv:
                            e.wait_ge(s_dv, pv)
                        for i in range(4):
                            tau = grp * 4 + i
                            j = tau // 4
                            w_ = tau % 4
                            Hk = j % 2
                            vt = vb[64 * Hk:64 * Hk + 64,
                                    512 * (j // 2) + 128 * w_:
                                    512 * (j // 2) + 128 * (w_ + 1)]
                            ident = i64d[64 * Hk:64 * Hk + 64, :]
                            mmi = e.matmul(bank[:, 64 * i:64 * (i + 1)], vt,
                                           ident, start=True, stop=True,
                                           is_transpose=True)
                            if i == 3:
                                mmi.then_inc(s_te, 1)
                    C.te += 1
                    tr = C.te
                    @on("vector")
                    def _(e, grp=grp, bank=bank, tr=tr, og=ones_gc):
                        e.wait_ge(s_te, tr)
                        e.wait_ge(s_gc, og)
                        dst = vp[:].rearrange("p (t c) -> p t c", c=65) \
                            [:, grp * 4:(grp + 1) * 4, 0:64]
                        dstl = vplo[:].rearrange("p (t c) -> p t c", c=65) \
                            [:, grp * 4:(grp + 1) * 4, 0:64]
                        src = bank[:, 0:256].rearrange("p (t c) -> p t c", c=64)
                        e.tensor_copy(dst, src).then_inc(s_dv, 1)
                        e.scalar_tensor_tensor(dstl, src, 1.0, dst,
                                               OP.mult, OP.subtract) \
                            .then_inc(s_dv, 1)
                    C.dv += 2
                    mk["vp_ev_dv"][grp] = C.dv
                vp_dv = C.dv
                mk["pf_free_dv"] = vp_dv
                if U0:
                    tap("vprime", vp[:])
                ckpt("vprime", u)

                scmB = {}
                elo_dv = {}
                expB = {}
                avB = {}
                augdv = {}
                rank1te = {}
                sumsdv = {}
                zdv = {-2: mk["pav_seed_dv"], -1: mk["pav_seed_dv"]}
                zsh_carry = mk["ztmp_last_gp"]
                zsh = {}
                amm_te = {}
                ared_dv = {("slot", 0): vp_dv, ("slot", 1): vp_dv,
                           ("slot", 2): vp_dv}
                slot_last_te = {1: 0, 2: 0}
                slot_last_dv = {1: 0, 2: 0}
                ABANKS = {0: pE, 1: pC, 2: pD}

                def emit_A_mm(sigma, i, sl=0, xte=0, xdv=0):
                    bank = ABANKS[sl]
                    prev = ared_dv.get(("slot", sl), 0)
                    @on("tensor")
                    def _(e, sigma=sigma, i=i, bank=bank, prev=prev,
                          gp=shuf_gp, gc=shuf_gc, xte=xte, xdv=xdv):
                        if sigma == 0 and i == 0:
                            e.wait_ge(s_gp, gp)
                            e.wait_ge(s_gc, gc)
                        if xte:
                            e.wait_ge(s_te, xte)
                        if max(prev, xdv):
                            e.wait_ge(s_dv, max(prev, xdv))
                        e.matmul(bank[:, 0:512],
                                 QAhi[0:128, 128 * sigma:128 * (sigma + 1)],
                                 KAhi[0:128, 512 * i:512 * (i + 1)],
                                 start=True, stop=True).then_inc(s_te, 1)
                    C.te += 1
                    amm_te[(sigma, i)] = C.te
                    if sl:
                        slot_last_te[sl] = C.te
                    @on("vector")
                    def _(e, sigma=sigma, i=i, bank=bank, mm=amm_te[(sigma, i)]):
                        e.wait_ge(s_te, mm)
                        e.tensor_reduce(macc[:, i:i + 1], bank[:, 0:512], AX,
                                        OP.max).then_inc(s_dv, 1)
                    C.dv += 1
                    ared_dv[(sigma, i)] = C.dv
                    ared_dv[("slot", sl)] = C.dv
                    if sl:
                        slot_last_dv[sl] = C.dv

                def emit_A_merge(sigma):
                    dve_fence()
                    @on("vector")
                    def _(e, sigma=sigma):
                        e.tensor_reduce(negm[:, 0:1], macc[:], AX, OP.max) \
                            .then_inc(s_dv, 1)
                    C.dv += 1
                    dve_fence()
                    @on("vector")
                    def _(e, sigma=sigma):
                        e.tensor_scalar(negm[:, 0:1], negm[:, 0:1], 4.0, -1.0,
                                        OP.add, OP.mult).then_inc(s_dv, 1)
                    C.dv += 1
                    dve_fence()
                    kc = sigma % 4
                    @on("vector")
                    def _(e, kc=kc):
                        e.tensor_copy(negp[:, 2 * kc:2 * kc + 1], negm[:, 0:1]) \
                            .then_inc(s_dv, 1)
                    C.dv += 1
                    dve_fence()
                    @on("vector")
                    def _(e, kc=kc):
                        e.scalar_tensor_tensor(negp[:, 2 * kc + 1:2 * kc + 2],
                                               negm[:, 0:1],
                                               1.0, negp[:, 2 * kc:2 * kc + 1],
                                               OP.mult, OP.subtract) \
                            .then_inc(s_dv, 1)
                    C.dv += 1
                    dve_fence()
                    return C.dv

                def emit_aug(b):
                    evs = []
                    for k_ in range(4):
                        sigma = 4 * b + k_
                        mg = amerge[sigma]
                        @on("tensor")
                        def _(e, k_=k_, mg=mg, bar=C.te):
                            e.wait_ge(s_te, bar)
                            e.wait_ge(s_dv, mg)
                            e.matmul(pFb[0:2, 256 * k_:256 * k_ + 128],
                                     negp[:, 2 * k_:2 * k_ + 2], i128b[:],
                                     start=True, stop=True,
                                     is_transpose=True, skip_group_check=True) \
                                .then_inc(s_te, 1)
                        C.te += 1
                        evs.append(C.te)
                    @on("vector")
                    def _(e, b=b, evs=tuple(evs)):
                        e.wait_ge(s_te, evs[-1])
                        for k_ in range(4):
                            sigma = 4 * b + k_
                            e.tensor_copy(QAlo[64:66,
                                               128 * sigma:128 * (sigma + 1)],
                                          pFb[0:2, 256 * k_:256 * k_ + 128]) \
                                .then_inc(s_dv, 1)
                    C.dv += 4
                    augdv[b] = C.dv

                amerge = {}

                def drain_A(n, sl=0, xte=0, xdv=0):
                    for _ in range(n):
                        if not aqueue:
                            return
                        kind, a1, a2 = aqueue.pop(0)
                        if kind == "mm":
                            emit_A_mm(a1, a2, sl=sl, xte=xte, xdv=xdv)
                            xte = xdv = 0
                        elif kind == "mrg":
                            amerge[a1] = emit_A_merge(a1)
                        else:
                            emit_aug(a1)

                aqueue = []

                def queue_A_for_sb(b):
                    if b > 15:
                        return
                    for k_ in range(4):
                        sigma = 4 * b + k_
                        for i in range(16):
                            aqueue.append(("mm", sigma, i))
                        aqueue.append(("mrg", sigma, 0))
                    aqueue.append(("aug", b, 0))

                queue_A_for_sb(0)
                rot = 0
                while aqueue:
                    kind = aqueue[0][0]
                    drain_A(1, sl=(rot % 3) if kind == "mm" else 0)
                    if kind == "mm":
                        rot += 1
                queue_A_for_sb(1)
                queue_A_for_sb(2)
                queue_A_for_sb(3)

                SLOTB = [pA[:, 0:512], pA[:, 512:1024],
                         pB[:, 0:512], pB[:, 512:1024]]
                def emit_av3(b, tau, pav):
                    sl_b = 1 if b % 2 == 0 else 2
                    bx_te = slot_last_te[sl_b] if tau == 0 else 0
                    bx_dv = slot_last_dv[sl_b] if tau == 0 else 0
                    @on("tensor")
                    def _(e, b=b, tau=tau, pav=pav, xa=expB[(b, tau)],
                          lo=elo_dv[(b, tau)], bx_te=bx_te, bx_dv=bx_dv):
                        if bx_te:
                            e.wait_ge(s_te, bx_te)
                        if bx_dv:
                            e.wait_ge(s_dv, bx_dv)
                        e.wait_ge(s_ac, xa)
                        e.wait_ge(s_gc, lo)
                        sc_ = 512 * (tau % 4)
                        e.matmul(pav[0:65, 0:512],
                                 vp[:, 65 * tau:65 * tau + 65],
                                 etslh[:, sc_:sc_ + 512],
                                 start=(tau == 0), stop=False,
                                 skip_group_check=True)
                        e.matmul(pav[0:65, 0:512],
                                 vp[:, 65 * tau:65 * tau + 65],
                                 etsll[:, sc_:sc_ + 512],
                                 start=False, stop=False,
                                 skip_group_check=True)
                        e.matmul(pav[0:65, 0:512],
                                 vplo[:, 65 * tau:65 * tau + 65],
                                 etslh[:, sc_:sc_ + 512],
                                 start=False, stop=(tau == 63),
                                 skip_group_check=True).then_inc(s_te, 1)
                    C.te += 1
                    avB[(b, tau)] = C.te

                for b in range(16):
                    pav = pC if b % 2 == 0 else pD
                    for tau in range(64):
                        slot = SLOTB[tau % 4]
                        slot_free = expB.get((b, tau - 4), 0) if tau >= 4 \
                            else (expB.get((b - 1, 60 + tau), 0) if b > 0 else 0)
                        @on("tensor")
                        def _(e, b=b, tau=tau, slot=slot, sf=slot_free,
                              aug=augdv[b], pvf=zdv[b - 2] if tau == 0 else 0):
                            if tau == 0:
                                e.wait_ge(s_dv, aug)
                            if pvf:
                                e.wait_ge(s_dv, pvf)
                            if sf:
                                e.wait_ge(s_ac, sf)
                            qs = QAhi[0:128, 512 * b:512 * (b + 1)]
                            ql = QAlo[0:128, 512 * b:512 * (b + 1)]
                            kh = KAhi[0:128, 128 * tau:128 * (tau + 1)]
                            kl = KAlo[0:128, 128 * tau:128 * (tau + 1)]
                            e.matmul(slot, kl, qs, start=True, stop=False)
                            e.matmul(slot, kh, ql, start=False, stop=True,
                                     skip_group_check=True).then_inc(s_te, 1)
                        C.te += 1
                        scmB[(b, tau)] = C.te
                        ets_need = avB.get((b, tau - 4), 0) if tau >= 4 \
                            else (avB.get((b - 1, 60 + tau), 0) if b > 0 else 0)
                        loev = elo_dv.get((b, tau - 4), 0) if tau >= 4 \
                            else (elo_dv.get((b - 1, 60 + tau), 0)
                                  if b > 0 else 0)
                        @on("scalar")
                        def _(e, b=b, tau=tau, slot=slot, mm=scmB[(b, tau)],
                              ets=ets_need, loev=loev):
                            e.wait_ge(s_te, max(mm, ets))
                            if loev:
                                e.wait_ge(s_gc, loev)
                            sc_ = 512 * (tau % 4)
                            e.activation(etsl[:, sc_:sc_ + 512], slot, AF.Exp,
                                         bias=0.0, scale=0.125) \
                                .then_inc(s_ac, 1)
                            e.activation(etslh[:, sc_:sc_ + 512],
                                         etsl[:, sc_:sc_ + 512], AF.Identity,
                                         bias=0.0, scale=1.0).then_inc(s_ac, 1)
                        C.ac += 2
                        expB[(b, tau)] = C.ac
                        @on("gpsimd")
                        def _(e, b=b, tau=tau, xa=expB[(b, tau)]):
                            e.wait_ge(s_ac, xa)
                            sc_ = 512 * (tau % 4)
                            e.tensor_tensor(etsll[:, sc_:sc_ + 512],
                                            etsl[:, sc_:sc_ + 512],
                                            etslh[:, sc_:sc_ + 512],
                                            OP.subtract).then_inc(s_gc, 1)
                        C.gc += 1
                        elo_dv[(b, tau)] = C.gc
                        if tau >= 3:
                            emit_av3(b, tau - 3, pav)
                        drain_A(1)
                        if tau in (52, 54, 56, 58, 60):
                            alt_sl = 2 if b % 2 == 0 else 1
                            drain_A(1, sl=alt_sl,
                                    xte=avB.get((b - 1, 63), 0),
                                    xdv=zdv.get(b - 1, 0))
                    emit_av3(b, 61, pav)
                    emit_av3(b, 62, pav)
                    emit_av3(b, 63, pav)
                    @on("vector")
                    def _(e, b=b, av=avB[(b, 63)], pav=pav):
                        e.wait_ge(s_te, av)
                        e.tensor_copy(srow[64:65, 0:512], pav[64:65, 0:512]) \
                            .then_inc(s_dv, 1)
                    C.dv += 1
                    sumsdv[b] = C.dv
                    @on("tensor")
                    def _(e, b=b, sd=sumsdv[b], bar=C.te,
                          rb_free=mk["rbc_read_dv"].get(b - 1, 0)):
                        e.wait_ge(s_te, bar)
                        e.wait_ge(s_dv, max(sd, rb_free))
                        e.matmul(pF[0:64, 0:512], onesb[64:65, 0:64],
                                 srow[64:65, 0:512], start=True, stop=True) \
                            .then_inc(s_te, 1)
                    C.te += 1
                    rank1te[b] = C.te
                    @on("vector")
                    def _(e, b=b, r1=rank1te[b]):
                        e.wait_ge(s_te, r1)
                        e.reciprocal(rbc[:, 0:512], pF[0:64, 0:512]) \
                            .then_inc(s_dv, 1)
                    C.dv += 1
                    mk["rbc_read_dv"][b] = C.dv
                    dve_fence()
                    if b % 2 == 0:
                        @on("vector")
                        def _(e, b=b, pav=pav):
                            e.tensor_tensor(zb[0:64, 512 * (b // 2):
                                               512 * (b // 2 + 1)],
                                            pav[0:64, 0:512], rbc[:, 0:512],
                                            OP.mult).then_inc(s_dv, 1)
                        C.dv += 1
                        zdv[b] = C.dv
                        mk["rbc_read_dv"][b] = C.dv
                    else:
                        pp = 512 * ((b // 2) % 2)
                        @on("vector")
                        def _(e, b=b, pav=pav, pp=pp,
                              ztf=zsh.get(b - 2, zsh_carry)):
                            if ztf:
                                e.wait_ge(s_gp, ztf)
                            e.tensor_tensor(ztmp[:, pp:pp + 512],
                                            pav[0:64, 0:512], rbc[:, 0:512],
                                            OP.mult).then_inc(s_dv, 1)
                        C.dv += 1
                        zdv[b] = C.dv
                        mk["rbc_read_dv"][b] = C.dv
                        @on("gpsimd")
                        def _(e, b=b, pp=pp, zz=zdv[b]):
                            e.wait_ge(s_dv, zz)
                            e.dma_start(zb[64:128, 512 * (b // 2):
                                           512 * (b // 2 + 1)],
                                        ztmp[:, pp:pp + 512]) \
                                .then_inc(s_gp, 16)
                        C.gp += 16
                        zsh[b] = C.gp
                    queue_A_for_sb(b + 4)
                attn_dv = C.dv
                attn_gp = C.gp
                attn_ac = expB[(15, 63)]
                mk["ztmp_last_gp"] = zsh[15]
                mk["aug_free_te"] = scmB[(15, 63)]
                if U0:
                    tap("zbuf", zb[:])
                ckpt("attn", u)

                op_mm_of = {}
                for cc in range(8):
                    half = cc % 2
                    wslice, wt_ld = wst_load(half, wot_d[128 * cc:128 * (cc + 1), :])
                    @on("tensor")
                    def _(e, cc=cc, wslice=wslice, wt_ld=wt_ld, attn=attn_dv,
                          agp=attn_gp, aac=attn_ac):
                        e.wait_ge(s_ld, wt_ld)
                        if cc == 0:
                            e.wait_ge(s_dv, attn)
                            e.wait_ge(s_gp, agp)
                            e.wait_ge(s_ac, aac)
                        for b8 in range(8):
                            st, ob = divmod(b8, 2)
                            mmi = e.matmul(BANK8[b8],
                                           zb[:, 512 * cc + 128 * st:
                                              512 * cc + 128 * (st + 1)],
                                           wslice[:, 512 * ob:512 * (ob + 1)],
                                           start=(cc == 0), stop=False,
                                           skip_group_check=True)
                            if b8 == 7:
                                mmi.then_inc(s_te, 1)
                    C.te += 1
                    op_mm_of[cc] = C.te
                    wst_user[half] = C.te
                @on("tensor")
                def _(e, og=ones_gc):
                    for b8 in range(8):
                        st, ob = divmod(b8, 2)
                        mmi = e.matmul(BANK8[b8], onesb[0:1, 0:128],
                                       bor[0:1, 512 * ob:512 * (ob + 1)],
                                       start=False, stop=True,
                                       skip_group_check=True)
                        if b8 == 7:
                            mmi.then_inc(s_te, 1)
                C.te += 1
                op_te = C.te

                for b8 in range(8):
                    st, ob = divmod(b8, 2)
                    col = 1024 * st + 512 * ob
                    @on("vector")
                    def _(e, b8=b8, col=col, mm=op_te):
                        e.wait_ge(s_te, mm)
                        e.scalar_tensor_tensor(zb[:, col:col + 512], BANK8[b8],
                                               1.0, x_tm[:, col:col + 512],
                                               OP.mult, OP.add).then_inc(s_dv, 1)
                    C.dv += 1
                opev_dv = C.dv
                ckpt("outproj", u)
                if U0:
                    tap("resid1", zb[:])

                ln1_dv = opev_dv
                for st in range(4):
                    ln1_dv = emit_ln(zb[:, 1024 * st:1024 * (st + 1)],
                                     y1[:, 1024 * st:1024 * (st + 1)], ln1_dv)
                if U0:
                    tap("y1", y1[:])

                for cc in range(8):
                    trb = pF if cc % 2 == 0 else pE
                    @on("tensor")
                    def _(e, cc=cc, trb=trb, ln1=ln1_dv,
                          pf=mk["pf_free_dv"] if cc < 2 else 0,
                          pv=mk["trev_dv"].get(("y", cc - 2), 0)):
                        e.wait_ge(s_dv, max(ln1, pf, pv))
                        for st in range(4):
                            mmi = e.matmul(trb[:, 128 * st:128 * (st + 1)],
                                           y1[:, 1024 * st + 128 * cc:
                                              1024 * st + 128 * (cc + 1)],
                                           i128f[:], start=True, stop=True,
                                           is_transpose=True)
                            if st == 3:
                                mmi.then_inc(s_te, 1)
                    C.te += 1
                    tr = C.te
                    @on("vector")
                    def _(e, cc=cc, trb=trb, tr=tr):
                        e.wait_ge(s_te, tr)
                        e.tensor_copy(QAhi[:, 512 * cc:512 * (cc + 1)], trb[:]) \
                            .then_inc(s_dv, 1)
                        e.scalar_tensor_tensor(
                            QAhi[:, 4096 + 512 * cc:4096 + 512 * (cc + 1)],
                            trb[:], 1.0, QAhi[:, 512 * cc:512 * (cc + 1)],
                            OP.mult, OP.subtract).then_inc(s_dv, 1)
                    C.dv += 2
                    mk["trev_dv"][("y", cc)] = C.dv
                    mk["pf_free_dv"] = C.dv
                y1T_dv = C.dv

                had_mm_of = {}
                had_ev_dv = {}
                had_ev_ac = {}
                for ht in range(8):
                    half = ht % 2
                    r = ht % 4
                    quad = QUAD[r]
                    wslice, wt_ld = wst_load(half,
                                             hh_d[ht].rearrange("p cc h -> p (cc h)"),
                                             bf=True)
                    wsb = wslice.bitcast(BF16)
                    @on("tensor")
                    def _(e, ht=ht, quad=quad, wsb=wsb, wt_ld=wt_ld,
                          ydv=y1T_dv, opev=opev_dv,
                          pdv=had_ev_dv.get(ht - 4, 0), pac=had_ev_ac.get(ht - 4, 0)):
                        e.wait_ge(s_ld, wt_ld)
                        e.wait_ge(s_dv, max(ydv, opev, pdv))
                        if pac:
                            e.wait_ge(s_ac, pac)
                        n = 0
                        for hl in range(2):
                            for cc in range(8):
                                n += 1
                                mmi = e.matmul(
                                    quad, wsb[:, 128 * cc:128 * (cc + 1)],
                                    QAhi[:, 4096 * hl + 512 * cc:
                                         4096 * hl + 512 * (cc + 1)],
                                    start=(hl == 0 and cc == 0),
                                    stop=(hl == 1 and cc == 7),
                                    skip_group_check=True)
                                if n == 16:
                                    mmi.then_inc(s_te, 1)
                    C.te += 1
                    had_mm_of[ht] = C.te
                    wst_user[half] = C.te
                    if r < 2:
                        @on("vector")
                        def _(e, ht=ht, quad=quad, mm=had_mm_of[ht]):
                            e.wait_ge(s_te, mm)
                            e.tensor_copy(xh_fm[:, 512 * ht:512 * (ht + 1)], quad) \
                                .then_inc(s_dv, 1)
                        C.dv += 1
                        had_ev_dv[ht] = C.dv
                    else:
                        @on("scalar")
                        def _(e, ht=ht, quad=quad, mm=had_mm_of[ht]):
                            e.wait_ge(s_te, mm)
                            e.activation(xh_fm[:, 512 * ht:512 * (ht + 1)], quad,
                                         AF.Identity, bias=0.0, scale=1.0) \
                                .then_inc(s_ac, 1)
                        C.ac += 1
                        had_ev_ac[ht] = C.ac
                had_dv = C.dv
                had_ac = C.ac
                ckpt("had", u)

                for st in range(4):
                    for hg in range(2):
                        idx = st * 2 + hg
                        trb = pF if idx % 2 == 0 else pE
                        @on("tensor")
                        def _(e, st=st, hg=hg, trb=trb, hdv=had_dv, hac=had_ac,
                              pf=mk["pf_free_dv"] if idx < 2 else 0,
                              pv=mk["trev_dv"].get(("xh", idx - 2), 0)):
                            e.wait_ge(s_dv, max(hdv, pf, pv))
                            e.wait_ge(s_ac, hac)
                            for i in range(4):
                                ht = 4 * hg + i
                                mmi = e.matmul(trb[:, 128 * i:128 * (i + 1)],
                                               xh_fm[:, 512 * ht + 128 * st:
                                                    512 * ht + 128 * (st + 1)],
                                               i128f[:], start=True, stop=True,
                                               is_transpose=True)
                                if i == 3:
                                    mmi.then_inc(s_te, 1)
                        C.te += 1
                        tr = C.te
                        @on("vector")
                        def _(e, st=st, hg=hg, trb=trb, tr=tr):
                            e.wait_ge(s_te, tr)
                            e.tensor_copy(xh_tm[:, 1024 * st + 512 * hg:
                                                1024 * st + 512 * (hg + 1)],
                                          trb[:]).then_inc(s_dv, 1)
                        C.dv += 1
                        mk["trev_dv"][("xh", idx)] = C.dv
                        mk["pf_free_dv"] = C.dv
                xh_dv = C.dv
                ckpt("xh", u)
                xh_tr_te = C.te
                if U0:
                    tap("xh_tm", xh_tm)

                qx_dv = 0
                for st in range(4):
                    t0 = 1024 * st
                    @on("vector")
                    def _(e, t0=t0, pre=xh_dv):
                        e.wait_ge(s_dv, pre)
                        e.tensor_reduce(mcol[:], xh_tm[:, t0:t0 + 1024], AX,
                                        OP.max, apply_absolute_value=True) \
                            .then_inc(s_dv, 1)
                    C.dv += 1
                    dve_fence()
                    @on("vector")
                    def _(e):
                        e.tensor_scalar(mcol[:], mcol[:], EPS, None, OP.max) \
                            .then_inc(s_dv, 1)
                    C.dv += 1
                    dve_fence()
                    @on("vector")
                    def _(e, st=st):
                        e.tensor_scalar(sxc[:, st:st + 1], mcol[:], 1.0 / 7.0,
                                        None, OP.mult).then_inc(s_dv, 1)
                        e.reciprocal(rq[:], mcol[:]).then_inc(s_dv, 1)
                    C.dv += 2
                    dve_fence()
                    @on("vector")
                    def _(e, okte=had_mm_of[7]):
                        e.wait_ge(s_te, okte)
                        e.tensor_scalar(rq[:], rq[:], 7.0, None, OP.mult) \
                            .then_inc(s_dv, 1)
                    C.dv += 1
                    dve_fence()
                    @on("vector")
                    def _(e, t0=t0):
                        e.tensor_scalar(wfa, xh_tm[:, t0:t0 + 1024], rq[:], None,
                                        OP.mult).then_inc(s_dv, 1)
                    C.dv += 1
                    vdv = C.dv
                    @on("scalar")
                    def _(e, t0=t0, vdv=vdv):
                        e.wait_ge(s_dv, vdv)
                        e.activation(zb[:, 3072:4096], wfa, AF.Abs,
                                     bias=0.0, scale=1.0).then_inc(s_ac, 1)
                    C.ac += 1
                    asdv = C.ac
                    @on("vector")
                    def _(e, t0=t0, asdv=asdv):
                        e.wait_ge(s_ac, asdv)
                        zb1 = zb[:, 1024:2048]
                        zb2 = zb[:, 2048:3072]
                        e.tensor_scalar(zb[:, 0:1024], wfa, 0.0, 2.0,
                                        OP.is_ge, OP.mult).then_inc(s_dv, 1)
                        e.tensor_scalar(zb[:, 0:1024], zb[:, 0:1024], 1.0,
                                        None, OP.subtract).then_inc(s_dv, 1)
                        e.tensor_scalar(zb1, zb[:, 3072:4096], 0.5, None,
                                        OP.is_gt).then_inc(s_dv, 1)
                        cur, other = zb1, zb2
                        for kk in range(1, 7):
                            cmp = OP.is_gt if kk % 2 == 0 else OP.is_ge
                            e.scalar_tensor_tensor(other, zb[:, 3072:4096],
                                                   kk + 0.5, cur,
                                                   cmp, OP.add).then_inc(s_dv, 1)
                            cur, other = other, cur
                        e.tensor_tensor(qx_tm[:, t0:t0 + 1024], cur,
                                        zb[:, 0:1024], OP.mult) \
                            .then_inc(s_dv, 1)
                    C.dv += 10
                    qx_dv = C.dv
                if U0:
                    tap("sxc", sxc[:])

                qx_tr_last_te = 0
                pEb = pE[:].bitcast(BF16)
                for ic in range(8):
                    trb = pFb if ic % 2 == 0 else pEb
                    @on("tensor")
                    def _(e, ic=ic, trb=trb, qx=qx_dv,
                          pf=mk["pf_free_dv"] if ic < 2 else 0,
                          pv=mk["trev_dv"].get(("qx", ic - 2), 0)):
                        e.wait_ge(s_dv, max(qx, pf, pv))
                        for st in range(4):
                            mmi = e.matmul(trb[:, 128 * st:128 * (st + 1)],
                                           qx_tm[:, 1024 * st + 128 * ic:
                                                 1024 * st + 128 * (ic + 1)],
                                           i128b[:], start=True, stop=True,
                                           is_transpose=True)
                            if st == 3:
                                mmi.then_inc(s_te, 1)
                    C.te += 1
                    tr = C.te
                    qx_tr_last_te = C.te
                    @on("vector")
                    def _(e, ic=ic, trb=trb, tr=tr):
                        e.wait_ge(s_te, tr)
                        e.tensor_copy(qxTb[:, 512 * ic:512 * (ic + 1)],
                                      trb[:, 0:512]).then_inc(s_dv, 1)
                    C.dv += 1
                    mk["trev_dv"][("qx", ic)] = C.dv
                    mk["pf_free_dv"] = C.dv
                qxT_dv = C.dv
                if U0:
                    tap("qxT", qxT)
                ckpt("qx", u)

                fmm_of = {}
                for cc in range(8):
                    half = cc % 2
                    wslice, wt_ld = wst_load(half, tpt_d[cc], bf=True)
                    wsb2 = wslice.bitcast(BF16)
                    @on("tensor")
                    def _(e, cc=cc, wsb2=wsb2, wt_ld=wt_ld, qxd=qxT_dv,
                          hdv=had_dv, hac=had_ac):
                        e.wait_ge(s_ld, wt_ld)
                        e.wait_ge(s_dv, max(qxd, hdv))
                        if cc == 0:
                            e.wait_ge(s_ac, hac)
                        n = 0
                        for hl in range(2):
                            for b8 in range(8):
                                st, ob = divmod(b8, 2)
                                n += 1
                                mmi = e.matmul(
                                    BANK8[b8],
                                    qxTb[:, 512 * cc + 128 * st:
                                         512 * cc + 128 * (st + 1)],
                                    wsb2[:, 1024 * hl + 512 * ob:
                                         1024 * hl + 512 * (ob + 1)],
                                    start=(cc == 0 and hl == 0),
                                    stop=(cc == 7 and hl == 1),
                                    skip_group_check=True)
                                if n == 16:
                                    mmi.then_inc(s_te, 1)
                    C.te += 1
                    fmm_of[cc] = C.te
                    wst_user[half] = C.te
                fmm_te = C.te

                for b8 in range(8):
                    st, ob = divmod(b8, 2)
                    col = 1024 * st + 512 * ob
                    @on("vector")
                    def _(e, b8=b8, st=st, col=col, mm=fmm_te):
                        e.wait_ge(s_te, mm)
                        e.scalar_tensor_tensor(x_tm[:, col:col + 512], BANK8[b8],
                                               sxc[:, st:st + 1],
                                               y1[:, col:col + 512],
                                               OP.mult, OP.add).then_inc(s_dv, 1)
                    C.dv += 1
                ffnev_dv = C.dv
                ckpt("ffnev", u)
                mk["banks_free_dv"] = ffnev_dv
                mk["banks_free_ac"] = had_ac
                mk["ebuf0_ok_te"] = qx_tr_last_te
                mk["pav_seed_dv"] = ffnev_dv
                if U0:
                    tap("resid2", x_tm[:])

                ln2_dv = ffnev_dv
                for st in range(4):
                    ln2_dv = emit_ln(x_tm[:, 1024 * st:1024 * (st + 1)], wfa,
                                     ln2_dv, pre_st=C.st)
                    @on("sync")
                    def _(e, u=u, st=st, need=ln2_dv):
                        e.wait_ge(s_dv, need)
                        e.dma_start(y_d[512 * u + 128 * st:512 * u + 128 * (st + 1), :],
                                    wfa).then_inc(s_st, 16)
                    C.st += 16
                mk["xtm_free_dv"] = ln2_dv
                ckpt("unit", u)

        try:
            emit_all()
        except _Stop:
            pass

        @on("sync")
        def _(e, st_tgt=C.st, ld_tgt=C.ld):
            e.wait_ge(s_st, st_tgt)
            e.wait_ge(s_ld, ld_tgt)
        @on("gpsimd")
        def _(e, gp_tgt=C.gp):
            e.wait_ge(s_gp, gp_tgt)

        block = ctx.enter_context(nc.Block())

        @block.sync
        def _(e):
            for fn in plan["sync"]:
                fn(e)

        @block.gpsimd
        def _(e):
            for fn in plan["gpsimd"]:
                fn(e)

        @block.tensor
        def _(e):
            for fn in plan["tensor"]:
                fn(e)

        @block.vector
        def _(e):
            for fn in plan["vector"]:
                fn(e)

        @block.scalar
        def _(e):
            for fn in plan["scalar"]:
                fn(e)

    return nc


def _hadamard(size):
    h = np.array([[1.0]], dtype=np.float64)
    while h.shape[0] < size:
        h = np.block([[h, h], [h, -h]])
    return (h / np.sqrt(size)).astype(np.float32)


def host_inputs(L0_attn_x, L0_attn_Wqkv, L0_attn_bqkv, L0_attn_Wo, L0_attn_bo,
                L0_ffn_weight):
    import ml_dtypes
    x = np.ascontiguousarray(np.asarray(L0_attn_x, np.float32)
                             .reshape(16384, 1024))
    Wqkv = np.asarray(L0_attn_Wqkv, np.float32)
    bqkv = np.asarray(L0_attn_bqkv, np.float32)
    Wo = np.asarray(L0_attn_Wo, np.float32)
    bo = np.asarray(L0_attn_bo, np.float32)
    Wf = np.ascontiguousarray(np.asarray(L0_ffn_weight, np.float32))

    idx = np.empty(3072, np.int64)
    for ot in range(24):
        p_, seg = divmod(ot, 3)
        for r_ in range(128):
            Hh, d = divmod(r_, 64)
            idx[ot * 128 + r_] = (2 * p_ + Hh) * 192 + seg * 64 + d
    Wperm = Wqkv[idx]
    bperm = np.ascontiguousarray(bqkv[idx])
    Wt = Wperm.T
    wqkv4 = np.ascontiguousarray(
        Wt.reshape(8, 128, 24, 128).transpose(2, 1, 0, 3))
    wot = np.ascontiguousarray(Wo.T)
    H = _hadamard(1024)
    hh4 = np.ascontiguousarray(
        H.reshape(8, 128, 8, 128).transpose(2, 1, 0, 3)).astype(
            ml_dtypes.bfloat16)
    i128f = np.eye(128, dtype=np.float32)
    i128b = np.eye(128).astype(ml_dtypes.bfloat16)
    i64d = np.vstack([np.eye(64), np.eye(64)]).astype(np.float32)

    shared = {
        "wqkv": wqkv4, "bqkv": bperm, "wot": wot,
        "bo": np.ascontiguousarray(bo.reshape(1, 1024)),
        "hh": hh4, "wf": Wf, "i128f": i128f, "i128b": i128b, "i64d": i64d,
    }
    in_maps = []
    for c in range(8):
        m = dict(shared)
        m["x"] = np.ascontiguousarray(x[2048 * c:2048 * (c + 1)])
        in_maps.append(m)
    return in_maps


def kernel(**inputs):
    in_maps = host_inputs(**inputs)
    nc = build_program(taps=DEBUG_TAPS)
    res = run_bass_kernel_spmd(nc, in_maps, core_ids=list(range(8)))
    y = np.concatenate([np.asarray(r["y"]) for r in res.results], axis=0)
    return y.reshape(2, 8192, 1024).astype(np.float32)
